# version 19 (final)
# baseline (speedup 1.0000x reference)
import numpy as np
import ml_dtypes
import concourse.bass as bass
import concourse.mybir as mybir
from concourse.bass_utils import run_bass_kernel_spmd

F32 = mybir.dt.float32
BF16 = mybir.dt.bfloat16
AF = mybir.ActivationFunctionType
ALU = mybir.AluOpType
AXL = mybir.AxisListType

L = 2048
D = 1024
NB = 4
BT = 512
NCB = 4
NF = 256

BF = ml_dtypes.bfloat16



def _build_consts():
    c = {}
    c["eye"] = np.eye(128, dtype=np.float32).astype(BF)
    p = np.arange(128)[:, None]
    f = np.arange(128)[None, :]
    mask_sl_neg = np.where(f < p, -1.0, 0.0).astype(np.float32)
    mask_ui = np.where(p <= f, 1.0, 0.0).astype(np.float32)
    c["masks"] = np.concatenate([mask_sl_neg, mask_ui], axis=1).astype(BF)
    onesl2 = np.zeros((128, 32), np.float32)
    for g in range(8):
        onesl2[:, 4 * g + (g // 2)] = 1.0
    c["onesl2"] = onesl2.astype(BF)
    oh4 = np.zeros((4, 512), np.float32)
    for h in range(4):
        oh4[h, 128 * h:128 * (h + 1)] = 1.0
    c["onehot4"] = oh4
    c["ones1"] = np.ones((1, 128), np.float32)
    c["ones1b"] = np.ones((1, 128), BF)
    c["epsc"] = np.tile(np.array([1e-5, 1e-6, 1.0 / 256.0, -1.0, -0.5, 0.0], np.float32), (128, 1))
    s = np.arange(NF)
    b = np.arange(128)
    ang = 2.0 * np.pi * np.outer(s, b) / NF
    CT0 = np.cos(ang)
    CT1 = -np.sin(ang)
    CT1[:, 0] = np.cos(np.pi * s)
    c["dft_f"] = np.concatenate([CT0, CT1], axis=1).astype(BF)
    su = 128 + np.arange(128)
    angi = 2.0 * np.pi * np.outer(b, su) / NF
    Ire = 2.0 * np.cos(angi) / NF
    Ire[0, :] = 1.0 / NF
    Iim = -2.0 * np.sin(angi) / NF
    Iim[0, :] = np.cos(np.pi * su) / NF
    c["dft_ir"] = Ire.astype(BF)
    c["dft_ii"] = Iim.astype(BF)
    for name, taps in (("l", 31), ("s", 3)):
        d = (taps - 1) - np.arange(taps)
        angw = 2.0 * np.pi * np.outer(d, b) / NF
        wre = np.cos(angw)
        wim = -np.sin(angw)
        wnyq = np.cos(np.pi * d)
        Wa = wre.copy()
        Wbc = wim.copy(); Wbc[:, 0] = 0.0
        Wd = wre.copy(); Wd[:, 0] = wnyq
        c["dftw_" + name] = np.concatenate([Wa, Wbc, Wd], axis=1).astype(BF)
    return c


def _prep_core_inputs(ins, bidx, consts):
    r8 = lambda w: np.ascontiguousarray(
        w.reshape(8, 128, w.shape[1]).transpose(1, 0, 2).reshape(128, -1))
    r8h = lambda w: np.ascontiguousarray(
        w.reshape(8, 128, 4, 256).transpose(1, 2, 0, 3).reshape(128, 8192))
    m = {}
    m["x"] = np.ascontiguousarray(ins["L0_attn_hidden_states"][bidx])
    m["crow"] = np.ascontiguousarray(ins["L0_c"][bidx:bidx + 1])
    m["ltrow"] = np.ascontiguousarray(np.asarray(ins["L0_attn_log_temp"])[None, :])
    m["wq"] = r8h(ins["L0_attn_Wq"])
    m["wk"] = r8h(ins["L0_attn_Wk"])
    m["wv"] = r8h(ins["L0_attn_Wv"])
    wbgl = np.concatenate([ins["L0_attn_Wb"], ins["L0_attn_Wg"], ins["L0_attn_Wl"]], axis=1)
    m["wbgl"] = r8(wbgl)
    m["wo"] = r8(ins["L0_attn_Wo"])
    m["wf_w"] = r8h(ins["L0_ffn_weight"])
    m["wf_a1"] = r8h(ins["L0_ffn_aux_weight1"])
    m["wf_a2"] = r8h(ins["L0_ffn_aux_weight2"])
    m["wmod"] = r8(ins["L0_wmod"])
    cw = np.zeros((128, 96), np.float32)
    for i, key in enumerate(("L0_attn_conv_q", "L0_attn_conv_k", "L0_attn_conv_v")):
        for g in range(8):
            cw[:, 32 * i + 4 * g: 32 * i + 4 * g + 4] = ins[key][128 * g:128 * (g + 1)]
    m["convw"] = cw
    m["firlT"] = np.ascontiguousarray(ins["L0_attn_fir_long"].reshape(1024, 31).T)
    m["firsT"] = np.ascontiguousarray(ins["L0_attn_fir_short"].reshape(1024, 3).T)
    bias20 = np.concatenate([np.zeros(4, np.float32),
                             np.asarray(ins["L0_attn_bg"], np.float32),
                             np.asarray(ins["L0_attn_bl"], np.float32)])
    m["bias20"] = np.tile(bias20, (128, 4)).astype(np.float32)
    m["rms_rep"] = np.tile(np.asarray(ins["L0_attn_rms_w"], np.float32), (128, 4))
    bias_f = np.zeros((128, 8), np.float32)
    for g in range(8):
        bias_f[:, g] = ins["L0_ffn_bias"][128 * g:128 * (g + 1)]
    m["bias_ffn"] = bias_f
    m.update(consts)
    return {k: np.ascontiguousarray(v) for k, v in m.items()}



class Ctr:
    def __init__(self, handle, step=1):
        self.h = handle
        self.step = step
        self.n = 0


SERIAL = False


class Sched:
    def __init__(self):
        self.items = []
        self.last = None

    def stage(self, eng, ctr, waits, emits):
        if SERIAL and self.last is not None:
            waits = list(waits) + [self.last]
        wmap = {}
        for mk in waits:
            if mk is None:
                continue
            c, v = mk
            if c not in wmap or wmap[c] < v:
                wmap[c] = v
        per_instr = ctr.step == 16
        if per_instr and ctr.n > 0:
            if ctr not in wmap or wmap[ctr] < ctr.n:
                wmap[ctr] = ctr.n
        ctr.n += ctr.step * (len(emits) if per_instr else 1)
        target = ctr.n
        wlist = list(wmap.items())

        def fn(e):
            for c, v in wlist:
                if v > 0:
                    e.wait_ge(c.h, v)
            last = None
            for g in emits:
                last = g(e)
                if per_instr:
                    last.then_inc(ctr.h, ctr.step)
            if not per_instr:
                last.then_inc(ctr.h, ctr.step)
        self.items.append((eng, fn))
        self.last = (ctr, target)
        return (ctr, target)



def build_nc():
    nc = bass.Bass("TRN2", target_bir_lowering=False)
    din = {}

    def dram(name, shape, dt=F32, kind="ExternalInput"):
        din[name] = nc.dram_tensor(name, shape, dt, kind=kind).ap()
        return din[name]

    x_h = dram("x", [L, D])
    out_h = dram("out", [L, D], kind="ExternalOutput")
    dram("crow", [1, D])
    dram("ltrow", [1, 4])
    for w in ("wq", "wk", "wv", "wo", "wf_w", "wf_a1", "wf_a2"):
        dram(w, [128, 8 * 1024])
    dram("wbgl", [128, 160])
    dram("wmod", [128, 8 * 6144])
    dram("convw", [128, 96])
    dram("firlT", [31, 1024])
    dram("firsT", [3, 1024])
    dram("bias20", [128, 80])
    dram("rms_rep", [128, 1024])
    dram("bias_ffn", [128, 8])
    dram("eye", [128, 128], BF16)
    dram("masks", [128, 256], BF16)
    dram("onesl2", [128, 32], BF16)
    dram("onehot4", [4, 512])
    dram("ones1", [1, 128])
    dram("ones1b", [1, 128], BF16)
    dram("epsc", [128, 6])
    dram("dft_f", [256, 256], BF16)
    dram("dft_ir", [128, 128], BF16)
    dram("dft_ii", [128, 128], BF16)
    dram("dftw_l", [31, 384], BF16)
    dram("dftw_s", [3, 384], BF16)

    sch = Sched()
    x_r = x_h.rearrange("(n p) f -> p n f", p=128)
    out_r = out_h.rearrange("(n p) f -> p n f", p=128)

    from contextlib import ExitStack
    _stack = ExitStack()
    with _stack:
        x_tm = _stack.enter_context(nc.sbuf_tensor("x_tm", [128, 16 * 1024], F32))
        wo_sb = _stack.enter_context(nc.sbuf_tensor("wo_sb", [128, 8192], BF16))
        S_sb = _stack.enter_context(nc.sbuf_tensor("S_sb", [128, 2048], BF16))
        eye_sb = _stack.enter_context(nc.sbuf_tensor("eye_sb", [128, 128], BF16))
        masks_sb = _stack.enter_context(nc.sbuf_tensor("masks_sb", [128, 256], BF16))
        onesl2_sb = _stack.enter_context(nc.sbuf_tensor("onesl2_sb", [128, 32], BF16))
        onehot4_sb = _stack.enter_context(nc.sbuf_tensor("onehot4_sb", [4, 512], F32))
        ones1_sb = _stack.enter_context(nc.sbuf_tensor("ones1_sb", [1, 128], F32))
        ones1b_sb = _stack.enter_context(nc.sbuf_tensor("ones1b_sb", [1, 128], BF16))
        epsc_sb = _stack.enter_context(nc.sbuf_tensor("epsc_sb", [128, 6], F32))
        bias20_sb = _stack.enter_context(nc.sbuf_tensor("bias20_sb", [128, 80], F32))
        rms_rep_sb = _stack.enter_context(nc.sbuf_tensor("rms_rep_sb", [128, 1024], F32))
        bias_ffn_sb = _stack.enter_context(nc.sbuf_tensor("bias_ffn_sb", [128, 8], F32))
        convw_sb = _stack.enter_context(nc.sbuf_tensor("convw_sb", [128, 96], F32))
        gates_fm = _stack.enter_context(nc.sbuf_tensor("gates_fm", [128, 48], F32))
        invtemp_rep = _stack.enter_context(nc.sbuf_tensor("invtemp_rep", [128, 4], F32))
        dftf_sb = _stack.enter_context(nc.sbuf_tensor("dftf_sb", [128, 512], BF16))
        dftir_sb = _stack.enter_context(nc.sbuf_tensor("dftir_sb", [128, 128], BF16))
        dftii_sb = _stack.enter_context(nc.sbuf_tensor("dftii_sb", [128, 128], BF16))
        whl_sb = _stack.enter_context(nc.sbuf_tensor("whl_sb", [128, 3072], BF16))
        whs_sb = _stack.enter_context(nc.sbuf_tensor("whs_sb", [128, 3072], BF16))
        cfm_sb = _stack.enter_context(nc.sbuf_tensor("cfm_sb", [128, 8], BF16))
        ltrow_sb = _stack.enter_context(nc.sbuf_tensor("ltrow_sb", [1, 4], F32))
        itrow_sb = _stack.enter_context(nc.sbuf_tensor("itrow_sb", [1, 4], F32))
        h_blk = _stack.enter_context(nc.sbuf_tensor("h_blk", [128, 4096], BF16))
        qh_blk = _stack.enter_context(nc.sbuf_tensor("qh_blk", [128, 4096], BF16))
        kh_blk = _stack.enter_context(nc.sbuf_tensor("kh_blk", [128, 4096], BF16))
        v_blk = _stack.enter_context(nc.sbuf_tensor("v_blk", [128, 4096], BF16))
        z_blk = _stack.enter_context(nc.sbuf_tensor("z_blk", [128, 8 * 516], BF16))
        wfm = z_blk[:, 0:1024]
        attnT = z_blk[:, 1024:1536]
        y_blk = _stack.enter_context(nc.sbuf_tensor("y_blk", [128, 4096], BF16))
        zh_q = _stack.enter_context(nc.sbuf_tensor("zh_q", [128, 32], BF16))
        zh_k = _stack.enter_context(nc.sbuf_tensor("zh_k", [128, 32], BF16))
        zh_v = _stack.enter_context(nc.sbuf_tensor("zh_v", [128, 32], BF16))
        w_str = _stack.enter_context(nc.sbuf_tensor("w_str", [128, 2048], BF16))
        wbgl_sb = _stack.enter_context(nc.sbuf_tensor("wbgl_sb", [128, 160], BF16))
        bgl_tm = _stack.enter_context(nc.sbuf_tensor("bgl_tm", [128, 80], F32))
        beta_blk = _stack.enter_context(nc.sbuf_tensor("beta_blk", [128, 16], F32))
        wg_t = _stack.enter_context(nc.sbuf_tensor("wg_t", [128, 16], F32))
        mix_b2 = _stack.enter_context(nc.sbuf_tensor("mix_b2", [128, 1024], F32))
        sq_scr = _stack.enter_context(nc.sbuf_tensor("sq_scr", [128, 256], F32))
        sm_m = _stack.enter_context(nc.sbuf_tensor("sm_m", [128, 16], F32))
        sm_e = _stack.enter_context(nc.sbuf_tensor("sm_e", [128, 48], F32))
        sm_s = _stack.enter_context(nc.sbuf_tensor("sm_s", [128, 16], F32))
        w1m = _stack.enter_context(nc.sbuf_tensor("w1m", [128, 16], F32))
        rs4 = _stack.enter_context(nc.sbuf_tensor("rs4", [4, 512], F32))
        rs_rep = _stack.enter_context(nc.sbuf_tensor("rs_rep", [128, 1024], BF16))
        dtiles = _stack.enter_context(nc.sbuf_tensor("dtiles", [128, 6144], BF16))
        ktm = dtiles[:, 0:1024]
        kbtm = dtiles[:, 1024:2048]
        vtm = dtiles[:, 2048:3072]
        vbtm = dtiles[:, 3072:4096]
        vtm_prev = dtiles[:, 4096:5120]
        u2_tm = dtiles[:, 5120:6144]

        Pp = _stack.enter_context(nc.sbuf_tensor("Pp", [128, 1024], BF16))
        Qp = _stack.enter_context(nc.sbuf_tensor("Qp", [128, 1024], BF16))
        Rt = _stack.enter_context(nc.sbuf_tensor("Rt", [128, 1024], BF16))
        xhat = _stack.enter_context(nc.sbuf_tensor("xhat", [128, 2048], BF16))
        yhat = _stack.enter_context(nc.sbuf_tensor("yhat", [128, 2048], BF16))
        ls_tm = _stack.enter_context(nc.sbuf_tensor("ls_tm", [128, 1024], BF16))
        ss_tm = _stack.enter_context(nc.sbuf_tensor("ss_tm", [128, 1024], BF16))
        ofm_sb = _stack.enter_context(nc.sbuf_tensor("ofm_sb", [128, 1024], BF16))
        mix_s = _stack.enter_context(nc.sbuf_tensor("mix_s", [128, 1024], F32))
        rsums = _stack.enter_context(nc.sbuf_tensor("rsums", [128, 4], F32))
        rstd2 = _stack.enter_context(nc.sbuf_tensor("rstd2", [128, 4], F32))
        o_rms = _stack.enter_context(nc.sbuf_tensor("o_rms", [128, 1024], BF16))
        st6 = _stack.enter_context(nc.sbuf_tensor("st6", [128, 48], F32))
        mv = _stack.enter_context(nc.sbuf_tensor("mv", [128, 8], F32))
        rstd_ln = _stack.enter_context(nc.sbuf_tensor("rstd_ln", [128, 4], F32))
        nmu_ln = _stack.enter_context(nc.sbuf_tensor("nmu_ln", [128, 4], F32))
        mod_sb = dtiles[0:1, :]
        ga_rep = ls_tm[:]
        bglf_sb = rs_rep[0:20, 0:512]
        wmod_p0 = z_blk[:, 0:1536]
        wmod_p1 = z_blk[:, 1536:3072]
        ps_tr = _stack.enter_context(nc.psum_tensor("ps_tr", [128, 1024], BF16))
        ps_g = _stack.enter_context(nc.psum_tensor("ps_g", [128, 512], F32))
        ps_p = _stack.enter_context(nc.psum_tensor("ps_p", [128, 512], F32))
        ps_q = _stack.enter_context(nc.psum_tensor("ps_q", [128, 512], F32))
        ps_u = _stack.enter_context(nc.psum_tensor("ps_u", [128, 1024], F32))
        ps_o = _stack.enter_context(nc.psum_tensor("ps_o", [128, 1024], F32))
        _pe = _stack.enter_context(nc.semaphore("s_pe"))
        _dve = _stack.enter_context(nc.semaphore("s_dve"))
        _act = _stack.enter_context(nc.semaphore("s_act"))
        _ld = _stack.enter_context(nc.semaphore("s_ld"))
        _ldx = _stack.enter_context(nc.semaphore("s_ldx"))
        _gld1 = _stack.enter_context(nc.semaphore("s_gld1"))
        _gld = _stack.enter_context(nc.semaphore("s_gld"))
        _st = _stack.enter_context(nc.semaphore("s_st"))
        block = _stack.enter_context(nc.Block())
        PE = Ctr(_pe); DVE = Ctr(_dve); ACT = Ctr(_act)
        LD = Ctr(_ld, 16); GLD = Ctr(_gld, 16); ST = Ctr(_st, 16)
        LDX = Ctr(_ldx, 16); GLD1 = Ctr(_gld1, 16)
        st = sch.stage

        m_consts = st("sync", LD, [], [
            lambda e: e.dma_start(eye_sb[:], din["eye"][:, :]),
            lambda e: e.dma_start(masks_sb[:], din["masks"][:, :]),
            lambda e: e.dma_start(onesl2_sb[:], din["onesl2"][:, :]),
            lambda e: e.dma_start(onehot4_sb[:], din["onehot4"][:, :]),
            lambda e: e.dma_start(ones1_sb[:], din["ones1"][:, :]),
            lambda e: e.dma_start(ones1b_sb[:], din["ones1b"][:, :]),
            lambda e: e.dma_start(epsc_sb[:], din["epsc"][:, :]),
            lambda e: e.dma_start(bias20_sb[:], din["bias20"][:, :]),
            lambda e: e.dma_start(rms_rep_sb[:], din["rms_rep"][:, :]),
            lambda e: e.dma_start(bias_ffn_sb[:], din["bias_ffn"][:, :]),
            lambda e: e.dma_start(convw_sb[:], din["convw"][:, :]),
            lambda e: e.dma_start(dftf_sb[:].rearrange("p (k b) -> p k b", k=2), din["dft_f"].rearrange("(k p) b -> p k b", p=128)),
            lambda e: e.dma_start(dftir_sb[:], din["dft_ir"][:, :]),
            lambda e: e.dma_start(dftii_sb[:], din["dft_ii"][:, :]),
            lambda e: e.dma_start(yhat[0:1, :].bitcast(F32), din["crow"][:, :]),
            lambda e: e.dma_start(ltrow_sb[:], din["ltrow"][:, :]),
            lambda e: e.dma_start(xhat[0:31, 0:384], din["dftw_l"][:, :]),
            lambda e: e.dma_start(xhat[0:3, 384:768], din["dftw_s"][:, :]),
        ])
        m_x = st("sync", LDX, [], [lambda e: e.dma_start(x_tm[:].rearrange("p (n f) -> p n f", n=16), x_r[:, :, :])])
        m_wo = st("gpsimd", GLD, [], [lambda e: e.dma_start(wo_sb[:], din["wo"][:, :])])
        m_wbgl = st("gpsimd", GLD, [], [lambda e: e.dma_start(wbgl_sb[:], din["wbgl"][:, :])])
        m_firT = st("gpsimd", GLD, [], [
            lambda e: e.dma_start(v_blk[0:31, 0:1024], din["firlT"][:, :]),
            lambda e: e.dma_start(v_blk[0:3, 1024:2048], din["firsT"][:, :]),
        ])

        wh_last = None
        for whbuf, taps, dofs, frofs in ((whl_sb, 31, 0, 0), (whs_sb, 3, 384, 1024)):
            for j in range(3):
                mm = st("tensor", PE, [m_consts, m_firT, wh_last], [
                    (lambda e, t=taps, d=dofs, fr=frofs, j=j, ns=ns: e.matmul(
                        ps_u[:, 512 * ns:512 * (ns + 1)],
                        xhat[0:t, d + 128 * j: d + 128 * (j + 1)],
                        v_blk[0:t, fr + 512 * ns: fr + 512 * (ns + 1)],
                        start=True, stop=True))
                    for ns in range(2)
                ])
                wh_last = st("vector", DVE, [mm], [
                    (lambda e, w=whbuf, j=j: e.tensor_copy(
                        w[:, 1024 * j:1024 * (j + 1)], ps_u[:, 0:1024]))
                ])
        m_wh = wh_last

        m_csilu = st("scalar", ACT, [m_consts], [
            lambda e: e.activation(ls_tm[0:1, 0:1024], yhat[0:1, :].bitcast(F32), AF.Silu)])
        m_it = st("scalar", ACT, [m_consts], [
            lambda e: e.activation(itrow_sb[:], ltrow_sb[:], AF.Exp, scale=epsc_sb[0:1, 3:4])])
        cfm_mm = st("tensor", PE, [m_csilu, m_it], [
            (lambda e, g=g: e.matmul(ps_o[:, g:g + 1], ls_tm[0:1, 128 * g:128 * (g + 1)],
                                     ones1b_sb[0:1, 0:1], start=True, stop=True))
            for g in range(8)
        ] + [lambda e: e.matmul(ps_o[:, 8:12], ones1_sb[0:1, :], itrow_sb[0:1, :],
                                start=True, stop=True)])
        m_cfm = st("vector", DVE, [cfm_mm], [
            lambda e: e.tensor_copy(cfm_sb[:], ps_o[:, 0:8]),
            lambda e: e.tensor_copy(invtemp_rep[:], ps_o[:, 8:12]),
        ])

        ps_slices = [ps_g[0:1, 0:512], ps_p[0:1, 0:512], ps_q[0:1, 0:512]]
        mod_ev_prev = None
        wm_mms = []
        for q in range(4):
            for g in range(8):
                pp = (wmod_p0, wmod_p1)[g % 2]
                waits = [m_cfm, m_wh]
                if len(wm_mms) >= 2:
                    waits.append(wm_mms[-2])
                if g == 0 and mod_ev_prev is not None:
                    waits.append(mod_ev_prev)
                dm = st("gpsimd", GLD, waits, [
                    (lambda e, pp=pp, g=g, q=q: e.dma_start(
                        pp, din["wmod"][:, 6144 * g + 1536 * q: 6144 * g + 1536 * (q + 1)]))
                ])
                mmw = [dm]
                if g == 0 and mod_ev_prev is not None:
                    mmw.append(mod_ev_prev)
                mm = st("tensor", PE, mmw, [
                    (lambda e, pp=pp, g=g, sl=sl: e.matmul(
                        ps_slices[sl], cfm_sb[:, g:g + 1], pp[:, 512 * sl:512 * (sl + 1)],
                        start=(g == 0), stop=(g == 7)))
                    for sl in range(3)
                ])
                wm_mms.append(mm)
            mod_ev_prev = st("vector", DVE, [wm_mms[-1]], [
                (lambda e, sl=sl, q=q: e.tensor_copy(
                    mod_sb[0:1, 1536 * q + 512 * sl: 1536 * q + 512 * (sl + 1)], ps_slices[sl]))
                for sl in range(3)
            ])
        m_mod = mod_ev_prev

        ga_row = mod_sb[0:1, 2048:3072]
        absrow = mix_s[0:1, :].bitcast(BF16)[0:1, 0:1024]
        sgnrow = yhat[0:1, 0:1024]
        zcrow = yhat[0:1, 1024:2048]
        m_abs = st("scalar", ACT, [m_mod, m_wh], [
            lambda e: e.activation(absrow, ga_row, AF.Abs),
            lambda e: e.activation(sgnrow, ga_row, AF.Sign),
        ])
        m_cl1 = st("vector", DVE, [m_abs], [
            lambda e: e.tensor_scalar_max(absrow, absrow, 0.25),
            lambda e: e.tensor_tensor(zcrow, sgnrow, sgnrow, ALU.mult),
        ])
        m_cl2 = st("vector", DVE, [m_cl1], [
            lambda e: e.tensor_scalar(zcrow, zcrow, -1.0, 1.0, ALU.mult, ALU.add)])
        m_cl3 = st("vector", DVE, [m_cl2], [
            lambda e: e.tensor_add(sgnrow, sgnrow, zcrow)])
        m_clamp = st("vector", DVE, [m_cl3], [
            lambda e: e.tensor_tensor(ga_row, absrow, sgnrow, ALU.mult)])
        gt_mm = st("tensor", PE, [m_clamp], [
            (lambda e, v=v, g=g: e.matmul(
                ps_o[:, 8 * v + g: 8 * v + g + 1],
                mod_sb[0:1, 1024 * v + 128 * g: 1024 * v + 128 * (g + 1)],
                ones1b_sb[0:1, 0:1], start=True, stop=True))
            for v in range(6) for g in range(8)
        ])
        m_gates0 = st("vector", DVE, [gt_mm], [
            lambda e: e.tensor_copy(gates_fm[:], ps_o[:, 0:48])])
        m_gbabs = st("scalar", ACT, [m_gates0], [
            lambda e: e.activation(st6[:, 0:8], gates_fm[:, 40:48], AF.Abs),
            lambda e: e.activation(st6[:, 8:16], gates_fm[:, 40:48], AF.Sign)])
        m_g1 = st("vector", DVE, [m_gbabs], [
            lambda e: e.tensor_scalar_add(gates_fm[:, 8:16], gates_fm[:, 8:16], 1.0),
            lambda e: e.tensor_scalar_add(gates_fm[:, 32:40], gates_fm[:, 32:40], 1.0),
            lambda e: e.tensor_scalar_max(st6[:, 0:8], st6[:, 0:8], 0.25),
            lambda e: e.tensor_tensor(st6[:, 16:24], st6[:, 8:16], st6[:, 8:16], ALU.mult),
        ])
        m_g2 = st("vector", DVE, [m_g1], [
            lambda e: e.tensor_scalar(st6[:, 16:24], st6[:, 16:24], -1.0, 1.0, ALU.mult, ALU.add)])
        m_g3 = st("vector", DVE, [m_g2], [
            lambda e: e.tensor_add(st6[:, 8:16], st6[:, 8:16], st6[:, 16:24])])
        m_gates = st("vector", DVE, [m_g3], [
            lambda e: e.tensor_tensor(gates_fm[:, 40:48], st6[:, 0:8], st6[:, 8:16], ALU.mult)])
        gar_mm = st("tensor", PE, [m_clamp, m_gates], [
            lambda e: e.matmul(ps_o[:, 0:512], ones1b_sb[0:1, :], mod_sb[0:1, 2048:2560],
                               start=True, stop=True),
            lambda e: e.matmul(ps_o[:, 512:1024], ones1b_sb[0:1, :], mod_sb[0:1, 2560:3072],
                               start=True, stop=True),
        ])
        m_gar = st("vector", DVE, [gar_mm], [
            lambda e: e.tensor_copy(ga_rep, ps_o[:, 0:1024])])
        m_wofold = st("vector", DVE, [m_gar, m_wo], [
            lambda e: e.tensor_tensor(
                wo_sb[:].rearrange("p (g f) -> p g f", g=8),
                wo_sb[:].rearrange("p (g f) -> p g f", g=8),
                ga_rep[:].unsqueeze(1).broadcast_to([128, 8, 1024]),
                ALU.mult)])
        m_hist0 = st("vector", DVE, [gt_mm, gar_mm], [
            lambda e: e.memset(zh_q[:], 0.0),
            lambda e: e.memset(zh_k[:], 0.0),
            lambda e: e.memset(zh_v[:], 0.0),
            lambda e: e.memset(S_sb[:], 0.0),
            lambda e: e.memset(vtm_prev, 0.0),
        ])

        state = {
            "m_h": m_consts, "m_x": m_x,
            "wstr_free": m_consts,
            "psg_free": m_gates, "pstr_free": m_wofold,
            "pspq_free": [m_mod, m_mod],
            "psu_free": m_wh, "pso_free": m_gar, "ofm_free": m_hist0,
            "m_S": m_hist0, "m_vtmprev": m_hist0, "m_mix_done": m_hist0,
            "m_clamp": m_clamp,
            "ln_free": m_gates,
            "rsrep_free": m_gates,
            "y_free": m_hist0,
            "z_free": m_hist0,
        }
        ln_gate_cols = [(1, 0), (4, 3)]

        def ln_and_h(b, which):
            cvec, svec = ln_gate_cols[which]
            xofs = 4096 * b
            stats0 = st("vector", DVE, [state["m_x"], state["ln_free"]], [
                (lambda e, n=n, hf=hf: e.bn_stats(
                    st6[:, 12 * n + 6 * hf: 12 * n + 6 * hf + 6],
                    x_tm[:, xofs + 1024 * n + 512 * hf: xofs + 1024 * n + 512 * (hf + 1)]))
                for n in range(4) for hf in range(2)
            ])
            stats = st("vector", DVE, [stats0], [
                (lambda e, n=n: e.bn_aggr(mv[:, 2 * n:2 * n + 2], st6[:, 12 * n:12 * n + 12]))
                for n in range(4)
            ])
            sq0 = st("scalar", ACT, [stats], [
                lambda e: e.activation(rstd_ln[:], mv[:, 1:8:2], AF.Ln, bias=epsc_sb[:, 0:1])])
            sq = st("scalar", ACT, [sq0], [
                lambda e: e.activation(rstd_ln[:], rstd_ln[:], AF.Exp, scale=epsc_sb[:, 4:5])])
            rst = st("vector", DVE, [sq], [
                lambda e: e.scalar_tensor_tensor(nmu_ln[:], mv[:, 0:8:2], -1.0, rstd_ln[:],
                                                 ALU.mult, ALU.mult),
            ])
            apply = st("scalar", ACT, [rst, state["y_free"]], [
                (lambda e, n=n: e.activation(
                    y_blk[:, 1024 * n:1024 * (n + 1)],
                    x_tm[:, xofs + 1024 * n: xofs + 1024 * (n + 1)],
                    AF.Identity, bias=nmu_ln[:, n:n + 1], scale=rstd_ln[:, n:n + 1]))
                for n in range(4)
            ])
            m_prev = state["pstr_free"]
            last_tmm = None
            for gp in range(4):
                tmm = st("tensor", PE, [apply, m_prev], [
                    (lambda e, g=g, n=n: e.transpose(
                        ps_tr[:, 512 * (g % 2) + 128 * n: 512 * (g % 2) + 128 * (n + 1)],
                        y_blk[:, 1024 * n + 128 * g: 1024 * n + 128 * g + 128], eye_sb[:]))
                    for g in (2 * gp, 2 * gp + 1) for n in range(4)
                ])
                m_prev = st("scalar", ACT, [tmm], [
                    (lambda e, g=g: e.activation(
                        h_blk[:, 512 * g:512 * (g + 1)],
                        ps_tr[:, 512 * (g % 2):512 * (g % 2) + 512],
                        AF.Identity, bias=gates_fm[:, 8 * svec + g: 8 * svec + g + 1],
                        scale=gates_fm[:, 8 * cvec + g: 8 * cvec + g + 1]))
                    for g in (2 * gp, 2 * gp + 1)
                ])
                last_tmm = tmm
            state["pstr_free"] = m_prev
            state["m_h"] = m_prev
            state["y_free"] = last_tmm
            state["ln_free"] = apply

        def project(b, wname, dst, zhist, cwofs):
            evs = []
            halves = [w_str[:, 0:1024], w_str[:, 1024:2048]]
            hfree = [state["wstr_free"], state["wstr_free"]]
            for qq in range(4):
                dmas = []
                for hb in range(2):
                    dmas.append(st("gpsimd", GLD, [hfree[hb]], [
                        (lambda e, wname=wname, qq=qq, hb=hb: e.dma_start(
                            halves[hb], din[wname][:, 2048 * qq + 1024 * hb:
                                                   2048 * qq + 1024 * (hb + 1)]))]))
                mm = None
                for hb in range(2):
                    for mh in range(2):
                        mc = 2 * qq + mh
                        pp = (ps_p, ps_q)[mh]
                        mm = st("tensor", PE, [dmas[hb], state["m_h"],
                                               state["pspq_free"][mh]], [
                            (lambda e, g=g, mh=mh, hb=hb, pp=pp: e.matmul(
                                pp[:, 0:512],
                                halves[hb][:, 256 * (g - 4 * hb) + 128 * mh:
                                           256 * (g - 4 * hb) + 128 * mh + 128],
                                h_blk[:, 512 * g:512 * (g + 1)],
                                start=(g == 0), stop=(g == 7)))
                            for g in range(4 * hb, 4 * hb + 4)
                        ])
                        if mh == 1:
                            hfree[hb] = mm
                for mh in range(2):
                    mc = 2 * qq + mh
                    pp = (ps_p, ps_q)[mh]
                    ev = st("scalar", ACT, [mm, state["z_free"]], [
                        (lambda e, m=mc, pp=pp: e.copy(
                            z_blk[:, 516 * m + 4: 516 * m + 516], pp[:, 0:512]))
                    ])
                    evs.append(ev)
                    state["pspq_free"][mh] = ev
            state["wstr_free"] = mm
            zv = z_blk[:].rearrange("p (g t) -> p g t", g=8)
            zhv = zhist[:].rearrange("p (g t) -> p g t", g=8)
            c_hi = st("vector", DVE, evs + [m_hist0, state["y_free"]], [
                (lambda e: e.tensor_max(zv[:, :, 0:4], zhv, zhv)),
            ])
            c_ts = st("vector", DVE, [c_hi], [
                (lambda e, g=g: e.tensor_scalar(
                    y_blk[:, 512 * g:512 * (g + 1)], z_blk[:, 516 * g + 4: 516 * g + 516],
                    convw_sb[:, cwofs + 4 * g + 3: cwofs + 4 * g + 4], None, ALU.mult))
                for g in range(8)
            ])
            conv = c_ts
            for j in range(3):
                conv = st("vector", DVE, [conv], [
                    (lambda e, g=g, j=j: e.scalar_tensor_tensor(
                        y_blk[:, 512 * g:512 * (g + 1)],
                        z_blk[:, 516 * g + j + 1: 516 * g + j + 513],
                        convw_sb[:, cwofs + 4 * g + j: cwofs + 4 * g + j + 1],
                        y_blk[:, 512 * g:512 * (g + 1)], ALU.mult, ALU.add))
                    for g in range(8)
                ])
            c_ho = st("vector", DVE, [conv], [
                (lambda e: e.tensor_max(zhv, zv[:, :, 512:516], zv[:, :, 512:516])),
            ])
            state["z_free"] = c_ho
            sl = st("scalar", ACT, [conv], [
                lambda e: e.activation(dst[:], y_blk[:], AF.Silu)])
            state["y_free"] = sl
            return sl

        def l2norm(src, m_src):
            sq = st("vector", DVE, [m_src, state["y_free"]], [
                lambda e: e.tensor_tensor(y_blk[:], src[:], src[:], ALU.mult)])
            smm = st("tensor", PE, [sq, state["psg_free"]], [
                (lambda e, g=g: e.matmul(ps_g[0:4, 0:512], onesl2_sb[:, 4 * g:4 * g + 4],
                                         y_blk[:, 512 * g:512 * (g + 1)],
                                         start=(g == 0), stop=(g == 7)))
                for g in range(8)
            ])
            state["y_free"] = smm
            rt0 = st("scalar", ACT, [smm], [
                lambda e: e.activation(rs4[0:4, 0:512], ps_g[0:4, 0:512],
                                       AF.Ln, bias=epsc_sb[0:4, 1:2])])
            rt = st("scalar", ACT, [rt0], [
                lambda e: e.activation(rs4[0:4, 0:512], rs4[0:4, 0:512],
                                       AF.Exp, scale=epsc_sb[0:4, 4:5])])
            state["psg_free"] = rt
            rc = rt
            marks = []
            for h in range(4):
                half = h % 2
                waits = [rc, state["pso_free"]]
                if len(marks) >= 2:
                    waits.append(marks[-2])
                rep = st("tensor", PE, waits, [
                    (lambda e, h=h, half=half: e.matmul(
                        ps_o[:, 512 * half:512 * (half + 1)],
                        onehot4_sb[:, 128 * h:128 * (h + 1)], rs4[0:4, 0:512],
                        start=True, stop=True))
                ])
                evr = st("scalar", ACT, [rep], [
                    (lambda e, half=half: e.copy(rs_rep[:, 512 * half:512 * (half + 1)],
                                                 ps_o[:, 512 * half:512 * (half + 1)]))
                ])
                ap = st("vector", DVE, [evr], [
                    (lambda e, h=h, half=half: e.tensor_tensor(
                        src[:, 1024 * h:1024 * (h + 1)].rearrange("p (c t) -> p c t", c=2),
                        src[:, 1024 * h:1024 * (h + 1)].rearrange("p (c t) -> p c t", c=2),
                        rs_rep[:, 512 * half:512 * (half + 1)].unsqueeze(1)
                        .broadcast_to([128, 2, 512]),
                        ALU.mult))
                ])
                marks.append(ap)
            state["pso_free"] = marks[-1]
            return marks[-1]

        def bgl_stage(m_after):
            mm = st("tensor", PE, [state["m_h"], m_wbgl, state["psg_free"]], [
                (lambda e, g=g: e.matmul(ps_g[0:20, 0:512], wbgl_sb[:, 20 * g:20 * (g + 1)],
                                         h_blk[:, 512 * g:512 * (g + 1)],
                                         start=(g == 0), stop=(g == 7)))
                for g in range(8)
            ])
            ev = st("scalar", ACT, [mm, m_after], [
                lambda e: e.copy(bglf_sb, ps_g[0:20, 0:512])])
            state["psg_free"] = ev
            tmm = st("tensor", PE, [ev, state["pstr_free"]], [
                (lambda e, tc=tc: e.transpose(ps_tr[:, 20 * tc:20 * (tc + 1)],
                                              bglf_sb[0:20, 128 * tc:128 * (tc + 1)],
                                              eye_sb[0:20, 0:20]))
                for tc in range(4)
            ])
            evb = st("vector", DVE, [tmm], [
                lambda e: e.tensor_add(bgl_tm[:], ps_tr[:, 0:80], bias20_sb[:])])
            state["pstr_free"] = evb
            bv = bgl_tm[:].rearrange("p (tc k) -> p tc k", tc=4)
            smev = sm_e[:].rearrange("p (tc h j) -> p tc h j", tc=4, h=4)
            lv = bv[:, :, 8:20].rearrange("p tc (h j) -> p tc h j", h=4)
            sig = st("scalar", ACT, [evb], [
                lambda e: e.activation(beta_blk[:].rearrange("p (tc h) -> p tc h", tc=4),
                                       bv[:, :, 0:4], AF.Sigmoid),
                lambda e: e.activation(wg_t[:].rearrange("p (tc h) -> p tc h", tc=4),
                                       bv[:, :, 4:8], AF.Sigmoid),
            ])
            smax0 = st("vector", DVE, [sig], [
                lambda e: e.tensor_reduce(sm_m[:].rearrange("p (tc h) -> p tc h", tc=4),
                                          lv, AXL.X, ALU.max)])
            smax1 = st("vector", DVE, [smax0], [
                lambda e: e.tensor_tensor(smev, lv,
                                          sm_m[:].rearrange("p (tc h) -> p tc h", tc=4)
                                          .unsqueeze(3).broadcast_to([128, 4, 4, 3]),
                                          ALU.subtract)])
            smax = st("vector", DVE, [smax1], [
                lambda e: e.tensor_tensor(smev, smev,
                                          invtemp_rep[:].unsqueeze(1).unsqueeze(3)
                                          .broadcast_to([128, 4, 4, 3]),
                                          ALU.mult)])
            ex = st("scalar", ACT, [smax], [
                lambda e: e.activation(sm_e[:], sm_e[:], AF.Exp)])
            red = st("vector", DVE, [ex], [
                lambda e: e.tensor_reduce(sm_s[:].rearrange("p (tc h) -> p tc h", tc=4),
                                          smev, AXL.X, ALU.add),
                lambda e: e.tensor_scalar(w1m[:], wg_t[:], -1.0, 1.0, ALU.mult, ALU.add),
            ])
            rcp0 = st("scalar", ACT, [red], [
                lambda e: e.activation(sm_m[:], sm_s[:], AF.Ln)])
            rcp = st("scalar", ACT, [rcp0], [
                lambda e: e.activation(sm_m[:], sm_m[:], AF.Exp, scale=epsc_sb[:, 3:4])])
            fin0 = st("vector", DVE, [rcp], [
                lambda e: e.tensor_tensor(smev, smev,
                                          w1m[:].rearrange("p (tc h) -> p tc h", tc=4)
                                          .unsqueeze(3).broadcast_to([128, 4, 4, 3]), ALU.mult)])
            fin = st("vector", DVE, [fin0], [
                lambda e: e.tensor_tensor(smev, smev,
                                          sm_m[:].rearrange("p (tc h) -> p tc h", tc=4)
                                          .unsqueeze(3).broadcast_to([128, 4, 4, 3]), ALU.mult)])
            return fin

        state["pend_tail"] = None

        def emit_tail():
            td = state["pend_tail"]
            state["pend_tail"] = None
            ot = st("tensor", PE, [td["orms"], state["pstr_free"]], [
                (lambda e, j=j: e.transpose(ps_tr[:, 128 * j:128 * (j + 1)],
                                            o_rms[:, 128 * j:128 * (j + 1)], eye_sb[:]))
                for j in range(8)
            ])
            oev = st("scalar", ACT, [ot, state["ofm_free"]], [
                lambda e: e.copy(ofm_sb[:], ps_tr[:, 0:1024])])
            state["pstr_free"] = oev
            wo_mm = st("tensor", PE, [oev, m_wofold, state["pso_free"]], [
                (lambda e, g=g, sl=sl: e.matmul(
                    ps_o[:, 512 * sl:512 * (sl + 1)], ofm_sb[:, 128 * g:128 * (g + 1)],
                    wo_sb[:, 1024 * g + 512 * sl: 1024 * g + 512 * (sl + 1)],
                    start=(g == 0), stop=(g == 7)))
                for sl in range(2) for g in range(8)
            ])
            state["ofm_free"] = wo_mm
            res = st("vector", DVE, [wo_mm], [
                (lambda e, b=td["b"], n=td["n"]: e.tensor_add(
                    x_tm[:, 4096 * b + 1024 * n: 4096 * b + 1024 * (n + 1)],
                    x_tm[:, 4096 * b + 1024 * n: 4096 * b + 1024 * (n + 1)],
                    ps_o[:, 0:1024]))
            ])
            state["pso_free"] = res
            state["m_x"] = res

        for b in range(NB):
            ln_and_h(b, 0)
            m_q = project(b, "wq", qh_blk, zh_q, 0)
            m_bgl = bgl_stage(state["rsrep_free"])
            m_k = project(b, "wk", kh_blk, zh_k, 32)
            m_v = project(b, "wv", v_blk, zh_v, 64)
            m_ql2 = l2norm(qh_blk, m_q)
            m_kl2 = l2norm(kh_blk, m_k)
            state["rsrep_free"] = m_kl2

            for n in range(NCB):
                cofs = 128 * n
                bc = 16 * n
                bb = 4 * n
                t1 = st("tensor", PE, [m_kl2, state["pstr_free"]], [
                    (lambda e, h=h, dc=dc, cofs=cofs: e.transpose(
                        ps_tr[:, 256 * h + 128 * dc: 256 * h + 128 * (dc + 1)],
                        kh_blk[:, 512 * (2 * h + dc) + cofs: 512 * (2 * h + dc) + cofs + 128],
                        eye_sb[:]))
                    for h in range(4) for dc in range(2)
                ])
                e1 = st("vector", DVE, [t1, m_bgl], [
                    lambda e: e.tensor_copy(ktm, ps_tr[:, 0:1024]),
                    (lambda e, bb=bb: e.tensor_tensor(
                        kbtm.rearrange("p (h t) -> p h t", h=4),
                        ps_tr[:, 0:1024].rearrange("p (h t) -> p h t", h=4),
                        beta_blk[:, bb:bb + 4].unsqueeze(2).broadcast_to([128, 4, 256]),
                        ALU.mult)),
                ])
                state["pstr_free"] = e1
                t2 = st("tensor", PE, [m_v, e1, state["m_vtmprev"]], [
                    (lambda e, h=h, dc=dc, cofs=cofs: e.transpose(
                        ps_tr[:, 256 * h + 128 * dc: 256 * h + 128 * (dc + 1)],
                        v_blk[:, 512 * (2 * h + dc) + cofs: 512 * (2 * h + dc) + cofs + 128],
                        eye_sb[:]))
                    for h in range(4) for dc in range(2)
                ])
                e2 = st("vector", DVE, [t2], [
                    lambda e: e.tensor_copy(vtm, ps_tr[:, 0:1024]),
                    (lambda e, bb=bb: e.tensor_tensor(
                        vbtm.rearrange("p (h t) -> p h t", h=4),
                        ps_tr[:, 0:1024].rearrange("p (h t) -> p h t", h=4),
                        beta_blk[:, bb:bb + 4].unsqueeze(2).broadcast_to([128, 4, 256]),
                        ALU.mult)),
                ])
                state["pstr_free"] = e2

                ffw = st("tensor", PE, [e2, state["m_vtmprev"], state["psu_free"],
                                        state["psg_free"], state["pspq_free"][0]], [
                    (lambda e, kc=kc, ns=ns: e.matmul(
                        ps_u[:, 512 * ns:512 * (ns + 1)],
                        dftf_sb[:, 256 * kc: 256 * kc + 128],
                        (vtm_prev if kc == 0 else vtm)[:, 512 * ns:512 * (ns + 1)],
                        start=(kc == 0), stop=(kc == 1)))
                    for ns in range(2) for kc in range(2)
                ] + [
                    (lambda e, kc=kc, ns=ns: e.matmul(
                        (ps_g if ns == 0 else ps_p)[:, 0:512],
                        dftf_sb[:, 256 * kc + 128: 256 * kc + 256],
                        (vtm_prev if kc == 0 else vtm)[:, 512 * ns:512 * (ns + 1)],
                        start=(kc == 0), stop=(kc == 1)))
                    for ns in range(2) for kc in range(2)
                ])
                exh = st("scalar", ACT, [ffw], [
                    lambda e: e.copy(xhat[:, 0:1024], ps_u[:, 0:1024]),
                    lambda e: e.copy(xhat[:, 1024:1536], ps_g[:, 0:512]),
                    lambda e: e.copy(xhat[:, 1536:2048], ps_p[:, 0:512]),
                ])
                state["psu_free"] = exh
                state["psg_free"] = exh
                state["pspq_free"][0] = exh
                state["m_vtmprev"] = st("vector", DVE, [e2, exh], [
                    lambda e: e.tensor_copy(vtm_prev, vtm)])
                mscr = mix_s[:].bitcast(BF16)
                yhat2 = mix_b2[:].bitcast(BF16)
                _yh = (yhat[:], yhat2)
                _whf = (whl_sb, whs_sb)
                _sc = (mscr[:, 0:1024], mscr[:, 1024:2048])
                cmw = [exh, m_wh, state["m_mix_done"], state["m_clamp"],
                       state["psu_free"]]
                cmf = [None, None]
                for f in range(2):
                    c1 = st("vector", DVE, (cmw if f == 0 else [cmf[0]]), [
                        (lambda e, f=f: e.tensor_tensor(
                            _yh[f][:, 0:1024], xhat[:, 0:1024], _whf[f][:, 0:1024], ALU.mult)),
                        (lambda e, f=f: e.tensor_tensor(
                            _sc[f], xhat[:, 1024:2048], _whf[f][:, 1024:2048], ALU.mult)),
                        (lambda e, f=f: e.tensor_tensor(
                            _yh[f][:, 1024:2048], xhat[:, 0:1024], _whf[f][:, 1024:2048], ALU.mult)),
                    ])
                    c2 = st("vector", DVE, [c1], [
                        (lambda e, f=f: e.tensor_sub(
                            _yh[f][:, 0:1024], _yh[f][:, 0:1024], _sc[f]))])
                    c3 = st("vector", DVE, [c2], [
                        (lambda e, f=f: e.tensor_tensor(
                            _sc[f], xhat[:, 1024:2048], _whf[f][:, 2048:3072], ALU.mult))])
                    cmf[f] = st("vector", DVE, [c3], [
                        (lambda e, f=f: e.tensor_add(
                            _yh[f][:, 1024:2048], _yh[f][:, 1024:2048], _sc[f]))])
                cm = cmf[1]
                im = st("tensor", PE, [cmf[0], state["psu_free"]], [
                    (lambda e, ns=ns, part=part: e.matmul(
                        ps_u[:, 512 * ns:512 * (ns + 1)],
                        (dftir_sb if part == 0 else dftii_sb)[:, :],
                        yhat[:, 1024 * part + 512 * ns: 1024 * part + 512 * (ns + 1)],
                        start=(part == 0), stop=(part == 1)))
                    for ns in range(2) for part in range(2)
                ])
                mfl = st("scalar", ACT, [im, state["m_mix_done"]], [
                    lambda e: e.copy(ls_tm[:], ps_u[:, 0:1024])])
                im2 = st("tensor", PE, [cm, mfl], [
                    (lambda e, ns=ns, part=part: e.matmul(
                        ps_u[:, 512 * ns:512 * (ns + 1)],
                        (dftir_sb if part == 0 else dftii_sb)[:, :],
                        yhat2[:, 1024 * part + 512 * ns: 1024 * part + 512 * (ns + 1)],
                        start=(part == 0), stop=(part == 1)))
                    for ns in range(2) for part in range(2)
                ])
                m_fir = st("scalar", ACT, [im2], [
                    lambda e: e.copy(ss_tm[:], ps_u[:, 0:1024])])
                state["psu_free"] = m_fir
                mA = st("vector", DVE, [cm, e2, m_bgl, state["m_clamp"]], [
                    (lambda e, h=h, n=n: e.tensor_scalar(
                        mix_s[:, 256 * h:256 * (h + 1)], vtm[:, 256 * h:256 * (h + 1)],
                        sm_e[:, 12 * n + 3 * h: 12 * n + 3 * h + 1], None, ALU.mult))
                    for h in range(4)
                ])
                mB = st("vector", DVE, [mA, m_fir], [
                    (lambda e, h=h, n=n: e.scalar_tensor_tensor(
                        mix_s[:, 256 * h:256 * (h + 1)], ss_tm[:, 256 * h:256 * (h + 1)],
                        sm_e[:, 12 * n + 3 * h + 1: 12 * n + 3 * h + 2],
                        mix_s[:, 256 * h:256 * (h + 1)], ALU.mult, ALU.add))
                    for h in range(4)
                ])
                mC = st("vector", DVE, [mB, mfl], [
                    (lambda e, h=h, n=n: e.scalar_tensor_tensor(
                        mix_s[:, 256 * h:256 * (h + 1)], ls_tm[:, 256 * h:256 * (h + 1)],
                        sm_e[:, 12 * n + 3 * h + 2: 12 * n + 3 * h + 3],
                        mix_s[:, 256 * h:256 * (h + 1)], ALU.mult, ALU.add))
                    for h in range(4)
                ])
                if state["pend_tail"] is not None:
                    emit_tail()
                g1 = st("tensor", PE, [m_kl2, state["psg_free"]], [
                    (lambda e, h=h, dc=dc, cofs=cofs: e.matmul(
                        ps_g[:, 128 * h:128 * (h + 1)],
                        kh_blk[:, 512 * (2 * h + dc) + cofs: 512 * (2 * h + dc) + cofs + 128],
                        kh_blk[:, 512 * (2 * h + dc) + cofs: 512 * (2 * h + dc) + cofs + 128],
                        start=(dc == 0), stop=(dc == 1)))
                    for h in range(4) for dc in range(2)
                ])
                p1 = st("vector", DVE, [g1, m_bgl], [
                    (lambda e, h=h, bb=bb: e.scalar_tensor_tensor(
                        Pp[:, 128 * h:128 * (h + 1)], ps_g[:, 128 * h:128 * (h + 1)],
                        beta_blk[:, bb + h: bb + h + 1], masks_sb[:, 0:128],
                        ALU.mult, ALU.mult))
                    for h in range(4)
                ])
                state["psg_free"] = p1
                q1t = st("tensor", PE, [p1, state["pstr_free"]], [
                    (lambda e, h=h: e.transpose(ps_tr[:, 128 * h:128 * (h + 1)],
                                                Pp[:, 128 * h:128 * (h + 1)], eye_sb[:]))
                    for h in range(4)
                ])
                q1e = st("vector", DVE, [q1t], [
                    lambda e: e.tensor_copy(Qp[:, 0:512], ps_tr[:, 0:512]),
                    lambda e: e.tensor_tensor(Rt[:, 0:512].rearrange("p (h t) -> p h t", h=4),
                                              ps_tr[:, 0:512].rearrange("p (h t) -> p h t", h=4),
                                              eye_sb[:].unsqueeze(1)
                                              .broadcast_to([128, 4, 128]),
                                              ALU.add),
                ])
                state["pstr_free"] = q1e

                pcur = qcur = rcur = 0
                m_chain = q1e
                m_sqprev = q1e
                for lev in range(1, 7):
                    pnx, qnx, rnx = 1 - pcur, 1 - qcur, 1 - rcur
                    sq_mm = st("tensor", PE, [m_sqprev, state["pspq_free"][0],
                                              state["pspq_free"][1]], [
                        (lambda e, h=h, pc=pcur, qc=qcur: e.matmul(
                            ps_p[:, 128 * h:128 * (h + 1)],
                            Qp[:, 512 * qc + 128 * h: 512 * qc + 128 * (h + 1)],
                            Pp[:, 512 * pc + 128 * h: 512 * pc + 128 * (h + 1)],
                            start=True, stop=True))
                        for h in range(4)
                    ] + ([
                        (lambda e, h=h, pc=pcur, qc=qcur: e.matmul(
                            ps_q[:, 128 * h:128 * (h + 1)],
                            Pp[:, 512 * pc + 128 * h: 512 * pc + 128 * (h + 1)],
                            Qp[:, 512 * qc + 128 * h: 512 * qc + 128 * (h + 1)],
                            start=True, stop=True))
                        for h in range(4)
                    ] if lev < 6 else []))
                    sq_ev = st("scalar", ACT, [sq_mm], [
                        (lambda e, pn=pnx: e.copy(Pp[:, 512 * pn:512 * (pn + 1)],
                                                  ps_p[:, 0:512])),
                    ] + ([
                        (lambda e, qn=qnx: e.copy(Qp[:, 512 * qn:512 * (qn + 1)],
                                                  ps_q[:, 0:512])),
                    ] if lev < 6 else []))
                    state["pspq_free"] = [sq_ev, sq_ev]
                    m_sqprev = sq_ev
                    pr_mm = st("tensor", PE, [sq_ev, m_chain, state["psg_free"]], [
                        (lambda e, h=h, pn=pnx, rc=rcur: e.matmul(
                            ps_g[:, 128 * h:128 * (h + 1)],
                            Pp[:, 512 * pn + 128 * h: 512 * pn + 128 * (h + 1)],
                            Rt[:, 512 * rc + 128 * h: 512 * rc + 128 * (h + 1)],
                            start=True, stop=True))
                        for h in range(4)
                    ])
                    pr_ev = st("vector", DVE, [pr_mm], [
                        (lambda e, rc=rcur, rn=rnx: e.tensor_add(
                            Rt[:, 512 * rn:512 * (rn + 1)], Rt[:, 512 * rc:512 * (rc + 1)],
                            ps_g[:, 0:512]))
                    ])
                    state["psg_free"] = pr_ev
                    m_chain = pr_ev
                    pcur, qcur, rcur = pnx, qnx, rnx

                w_mm = st("tensor", PE, [m_chain, e1, state["pspq_free"][0],
                                         state["pspq_free"][1]], [
                    (lambda e, h=h, dc=dc, rc=rcur: e.matmul(
                        (ps_p if (2 * h + dc) < 4 else ps_q)
                        [:, 128 * ((2 * h + dc) % 4):128 * ((2 * h + dc) % 4 + 1)],
                        kbtm[:, 256 * h + 128 * dc: 256 * h + 128 * (dc + 1)],
                        Rt[:, 512 * rc + 128 * h: 512 * rc + 128 * (h + 1)],
                        start=True, stop=True))
                    for h in range(4) for dc in range(2)
                ])
                w_ev = st("vector", DVE, [w_mm], [
                    lambda e: e.tensor_scalar(wfm[:, 0:512], ps_p[:, 0:512], -1.0, None, ALU.mult),
                    lambda e: e.tensor_scalar(wfm[:, 512:1024], ps_q[:, 0:512], -1.0, None, ALU.mult),
                ])
                state["pspq_free"] = [w_ev, w_ev]
                u2_mm = st("tensor", PE, [m_chain, e2, w_ev, state["m_S"],
                                          state["psu_free"]], [
                    mmfn for h in range(4) for mmfn in (
                        [(lambda e, h=h, rc=rcur: e.matmul(
                            ps_u[:, 256 * h:256 * (h + 1)],
                            Rt[:, 512 * rc + 128 * h: 512 * rc + 128 * (h + 1)],
                            vbtm[:, 256 * h:256 * (h + 1)], start=True, stop=False))]
                        + [(lambda e, h=h, dc=dc, cofs=cofs: e.matmul(
                            ps_u[:, 256 * h:256 * (h + 1)],
                            wfm[:, 256 * h + 128 * dc: 256 * h + 128 * (dc + 1)],
                            S_sb[:, 256 * (2 * h + dc): 256 * (2 * h + dc) + 256],
                            start=False, stop=(dc == 1)))
                           for dc in range(2)])
                ])
                u2_ev = st("scalar", ACT, [u2_mm], [
                    lambda e: e.copy(u2_tm[:, 0:512], ps_u[:, 0:512]),
                    lambda e: e.copy(u2_tm[:, 512:1024], ps_u[:, 512:1024]),
                ])
                state["psu_free"] = u2_ev

                at_mm = st("tensor", PE, [m_ql2, m_kl2, state["psg_free"]], [
                    (lambda e, h=h, dc=dc, cofs=cofs: e.matmul(
                        ps_g[:, 128 * h:128 * (h + 1)],
                        kh_blk[:, 512 * (2 * h + dc) + cofs: 512 * (2 * h + dc) + cofs + 128],
                        qh_blk[:, 512 * (2 * h + dc) + cofs: 512 * (2 * h + dc) + cofs + 128],
                        start=(dc == 0), stop=(dc == 1)))
                    for h in range(4) for dc in range(2)
                ])
                at_ev = st("vector", DVE, [at_mm], [
                    lambda e: e.tensor_tensor(attnT.rearrange("p (h t) -> p h t", h=4),
                                              ps_g[:, 0:512].rearrange("p (h t) -> p h t", h=4),
                                              masks_sb[:, 128:256].unsqueeze(1)
                                              .broadcast_to([128, 4, 128]),
                                              ALU.mult)
                ])
                state["psg_free"] = at_ev

                o_mm = st("tensor", PE, [m_ql2, u2_ev, at_ev, state["m_S"],
                                         state["pso_free"], state["m_mix_done"]], [
                    mmfn for h in range(4) for mmfn in (
                        [(lambda e, h=h, dc=dc, cofs=cofs: e.matmul(
                            ps_o[:, 256 * h:256 * (h + 1)],
                            qh_blk[:, 512 * (2 * h + dc) + cofs: 512 * (2 * h + dc) + cofs + 128],
                            S_sb[:, 256 * (2 * h + dc): 256 * (2 * h + dc) + 256],
                            start=(dc == 0), stop=False))
                           for dc in range(2)]
                        + [(lambda e, h=h: e.matmul(
                            ps_o[:, 256 * h:256 * (h + 1)], attnT[:, 128 * h:128 * (h + 1)],
                            u2_tm[:, 256 * h:256 * (h + 1)], start=False, stop=True))])
                ])

                mD = st("vector", DVE, [o_mm, mC], [
                    (lambda e, h=h, n=n: e.scalar_tensor_tensor(
                        mix_b2[:, 256 * h:256 * (h + 1)], ps_o[:, 256 * h:256 * (h + 1)],
                        wg_t[:, 4 * n + h: 4 * n + h + 1],
                        mix_s[:, 256 * h:256 * (h + 1)], ALU.mult, ALU.add))
                    for h in range(4)
                ])
                state["pso_free"] = mD
                m_supd = []
                for dc in range(2):
                    su_mm = st("tensor", PE, [u2_ev, e1, state["pso_free"]] + m_supd, [
                        (lambda e, h=h, dc=dc: e.matmul(
                            ps_o[:, 256 * h:256 * (h + 1)],
                            ktm[:, 256 * h + 128 * dc: 256 * h + 128 * (dc + 1)],
                            u2_tm[:, 256 * h:256 * (h + 1)], start=True, stop=True))
                        for h in range(4)
                    ])
                    su_ev = st("vector", DVE, [su_mm, state["m_S"]], [
                        (lambda e, dc=dc: e.tensor_tensor(
                            S_sb[:].rearrange("p (h d t) -> p h d t", h=4, d=2)[:, :, dc, :],
                            S_sb[:].rearrange("p (h d t) -> p h d t", h=4, d=2)[:, :, dc, :],
                            ps_o[:, 0:1024].rearrange("p (h t) -> p h t", h=4),
                            ALU.add))
                    ])
                    m_supd.append(su_ev)
                    state["pso_free"] = su_ev
                state["m_S"] = m_supd[-1]
                _sqd = [sq_scr[:], mix_s[:, 0:256], mix_s[:, 256:512], mix_s[:, 512:768]]
                mE = st("vector", DVE, [mD], [
                    (lambda e, h=h: e.scalar_tensor_tensor(
                        _sqd[h], mix_b2[:, 256 * h:256 * (h + 1)], 1.0,
                        mix_b2[:, 256 * h:256 * (h + 1)], ALU.mult, ALU.mult,
                        accum_out=rsums[:, h:h + 1]))
                    for h in range(4)
                ])
                rr0 = st("scalar", ACT, [mE], [
                    lambda e: e.activation(rstd2[:], rsums[:], AF.Ln,
                                           bias=epsc_sb[:, 0:1], scale=epsc_sb[:, 2:3])])
                rr = st("scalar", ACT, [rr0], [
                    lambda e: e.activation(rstd2[:], rstd2[:], AF.Exp, scale=epsc_sb[:, 4:5])])
                orms = st("vector", DVE, [rr], [
                    (lambda e, h=h: e.scalar_tensor_tensor(
                        o_rms[:, 256 * h:256 * (h + 1)],
                        mix_b2[:, 256 * h:256 * (h + 1)],
                        rstd2[:, h:h + 1], rms_rep_sb[:, 256 * h:256 * (h + 1)],
                        ALU.mult, ALU.mult))
                    for h in range(4)
                ])
                state["m_mix_done"] = orms
                state["pend_tail"] = {"orms": orms, "b": b, "n": n}

            emit_tail()
            ln_and_h(b, 1)
            ffn_srcs = [h_blk, qh_blk, kh_blk, y_blk]
            m_src_ready = state["m_h"]
            halves = [w_str[:, 0:1024], w_str[:, 1024:2048]]
            for fi, wname in enumerate(("wf_w", "wf_a1", "wf_a2")):
                src = ffn_srcs[fi]
                dstbuf = ffn_srcs[fi + 1]
                last_ev = None
                hfree = [state["wstr_free"], state["wstr_free"]]
                for qq in range(4):
                    dmas = []
                    for hb in range(2):
                        dmas.append(st("gpsimd", GLD, [hfree[hb]], [
                            (lambda e, wname=wname, qq=qq, hb=hb: e.dma_start(
                                halves[hb], din[wname][:, 2048 * qq + 1024 * hb:
                                                       2048 * qq + 1024 * (hb + 1)]))]))
                    mm = None
                    for hb in range(2):
                        for mh in range(2):
                            pp = (ps_p, ps_q)[mh]
                            mm = st("tensor", PE, [dmas[hb], m_src_ready,
                                                   state["pspq_free"][mh]], [
                                (lambda e, g=g, mh=mh, hb=hb, pp=pp, src=src: e.matmul(
                                    pp[:, 0:512],
                                    halves[hb][:, 256 * (g - 4 * hb) + 128 * mh:
                                               256 * (g - 4 * hb) + 128 * mh + 128],
                                    src[:, 512 * g:512 * (g + 1)],
                                    start=(g == 0), stop=(g == 7)))
                                for g in range(4 * hb, 4 * hb + 4)
                            ])
                            if mh == 1:
                                hfree[hb] = mm
                    for mh in range(2):
                        mc = 2 * qq + mh
                        pp = (ps_p, ps_q)[mh]
                        if fi < 2:
                            ev = st("scalar", ACT, [mm], [
                                (lambda e, m=mc, pp=pp, dst=dstbuf: e.copy(
                                    dst[:, 512 * m:512 * (m + 1)], pp[:, 0:512]))])
                        else:
                            ev = st("scalar", ACT, [mm, state["y_free"]], [
                                (lambda e, m=mc, pp=pp: e.activation(
                                    y_blk[:, 512 * m:512 * (m + 1)], pp[:, 0:512], AF.Tanh,
                                    bias=bias_ffn_sb[:, m:m + 1]))])
                            ev = st("vector", DVE, [ev], [
                                (lambda e, m=mc: e.tensor_scalar(
                                    y_blk[:, 512 * m:512 * (m + 1)],
                                    y_blk[:, 512 * m:512 * (m + 1)],
                                    gates_fm[:, 40 + m:41 + m], None, ALU.mult))])
                        state["pspq_free"][mh] = ev
                        last_ev = ev
                m_src_ready = last_ev
                state["wstr_free"] = mm
            state["y_free"] = m_src_ready
            for n in range(4):
                tm = st("tensor", PE, [m_src_ready, state["pstr_free"]], [
                    (lambda e, g=g, n=n: e.transpose(
                        ps_tr[:, 128 * g:128 * (g + 1)],
                        y_blk[:, 512 * g + 128 * n: 512 * g + 128 * (n + 1)], eye_sb[:]))
                    for g in range(8)
                ])
                ra = st("vector", DVE, [tm], [
                    (lambda e, b=b, n=n: e.tensor_add(
                        x_tm[:, 4096 * b + 1024 * n: 4096 * b + 1024 * (n + 1)],
                        x_tm[:, 4096 * b + 1024 * n: 4096 * b + 1024 * (n + 1)],
                        ps_tr[:, 0:1024]))
                ])
                state["pstr_free"] = ra
                state["m_x"] = ra
            st("sync", ST, [state["m_x"]], [
                (lambda e, b=b: e.dma_start(
                    out_r[:, 4 * b:4 * (b + 1), :],
                    x_tm[:, 4096 * b:4096 * (b + 1)].rearrange("p (n f) -> p n f", n=4)))])

        def fin(e):
            e.wait_ge(ST.h, ST.n)
        sch.items.append(("sync", fin))

        @block.tensor
        def _(t):
            for eng, fn in sch.items:
                if eng == "tensor":
                    fn(t)

        @block.vector
        def _(v):
            for eng, fn in sch.items:
                if eng == "vector":
                    fn(v)

        @block.scalar
        def _(s):
            for eng, fn in sch.items:
                if eng == "scalar":
                    fn(s)

        @block.gpsimd
        def _(g):
            for eng, fn in sch.items:
                if eng == "gpsimd":
                    fn(g)

        @block.sync
        def _(s):
            for eng, fn in sch.items:
                if eng == "sync":
                    fn(s)

    return nc


def kernel(**inputs):
    consts = _build_consts()
    nc = build_nc()
    in_maps = [_prep_core_inputs(inputs, b, consts) for b in range(8)]
    res = run_bass_kernel_spmd(nc, in_maps, core_ids=list(range(8)))
    return np.stack([r["out"] for r in res.results], axis=0).reshape(8, L, D)


# version 20
# speedup vs baseline: 1.0327x; 1.0327x over previous
import numpy as np
import ml_dtypes
import concourse.bass as bass
import concourse.mybir as mybir
from concourse.bass_utils import run_bass_kernel_spmd

F32 = mybir.dt.float32
BF16 = mybir.dt.bfloat16
AF = mybir.ActivationFunctionType
ALU = mybir.AluOpType
AXL = mybir.AxisListType

L = 2048
D = 1024
NB = 4
BT = 512
NCB = 4
NF = 256

BF = ml_dtypes.bfloat16



def _build_consts():
    c = {}
    c["eye"] = np.eye(128, dtype=np.float32).astype(BF)
    p = np.arange(128)[:, None]
    f = np.arange(128)[None, :]
    mask_sl_neg = np.where(f < p, -1.0, 0.0).astype(np.float32)
    mask_ui = np.where(p <= f, 1.0, 0.0).astype(np.float32)
    c["masks"] = np.concatenate([mask_sl_neg, mask_ui], axis=1).astype(BF)
    onesl2 = np.zeros((128, 32), np.float32)
    for g in range(8):
        onesl2[:, 4 * g + (g // 2)] = 1.0
    c["onesl2"] = onesl2.astype(BF)
    oh4 = np.zeros((4, 512), np.float32)
    for h in range(4):
        oh4[h, 128 * h:128 * (h + 1)] = 1.0
    c["onehot4"] = oh4
    c["ones1"] = np.ones((1, 128), np.float32)
    c["ones1b"] = np.ones((1, 128), BF)
    c["epsc"] = np.tile(np.array([1e-5, 1e-6, 1.0 / 256.0, -1.0, -0.5, 0.0], np.float32), (128, 1))
    s = np.arange(NF)
    b = np.arange(128)
    ang = 2.0 * np.pi * np.outer(s, b) / NF
    CT0 = np.cos(ang)
    CT1 = -np.sin(ang)
    CT1[:, 0] = np.cos(np.pi * s)
    c["dft_f"] = np.concatenate([CT0, CT1], axis=1).astype(BF)
    su = 128 + np.arange(128)
    angi = 2.0 * np.pi * np.outer(b, su) / NF
    Ire = 2.0 * np.cos(angi) / NF
    Ire[0, :] = 1.0 / NF
    Iim = -2.0 * np.sin(angi) / NF
    Iim[0, :] = np.cos(np.pi * su) / NF
    c["dft_ir"] = Ire.astype(BF)
    c["dft_ii"] = Iim.astype(BF)
    for name, taps in (("l", 31), ("s", 3)):
        d = (taps - 1) - np.arange(taps)
        angw = 2.0 * np.pi * np.outer(d, b) / NF
        wre = np.cos(angw)
        wim = -np.sin(angw)
        wnyq = np.cos(np.pi * d)
        Wa = wre.copy()
        Wbc = wim.copy(); Wbc[:, 0] = 0.0
        Wd = wre.copy(); Wd[:, 0] = wnyq
        c["dftw_" + name] = np.concatenate([Wa, Wbc, Wd], axis=1).astype(BF)
    return c


def _prep_core_inputs(ins, bidx, consts):
    r8 = lambda w: np.ascontiguousarray(
        w.reshape(8, 128, w.shape[1]).transpose(1, 0, 2).reshape(128, -1))
    r8h = lambda w: np.ascontiguousarray(
        w.reshape(8, 128, 4, 256).transpose(1, 2, 0, 3).reshape(128, 8192))
    m = {}
    m["x"] = np.ascontiguousarray(ins["L0_attn_hidden_states"][bidx])
    m["crow"] = np.ascontiguousarray(ins["L0_c"][bidx:bidx + 1])
    m["ltrow"] = np.ascontiguousarray(np.asarray(ins["L0_attn_log_temp"])[None, :])
    m["wq"] = r8h(ins["L0_attn_Wq"])
    m["wk"] = r8h(ins["L0_attn_Wk"])
    m["wv"] = r8h(ins["L0_attn_Wv"])
    wbgl = np.concatenate([ins["L0_attn_Wb"], ins["L0_attn_Wg"], ins["L0_attn_Wl"]], axis=1)
    m["wbgl"] = r8(wbgl)
    m["wo"] = r8(ins["L0_attn_Wo"])
    m["wf_w"] = r8h(ins["L0_ffn_weight"])
    m["wf_a1"] = r8h(ins["L0_ffn_aux_weight1"])
    m["wf_a2"] = r8h(ins["L0_ffn_aux_weight2"])
    m["wmod"] = r8(ins["L0_wmod"])
    cw = np.zeros((128, 96), np.float32)
    for i, key in enumerate(("L0_attn_conv_q", "L0_attn_conv_k", "L0_attn_conv_v")):
        for g in range(8):
            cw[:, 32 * i + 4 * g: 32 * i + 4 * g + 4] = ins[key][128 * g:128 * (g + 1)]
    m["convw"] = cw
    m["firlT"] = np.ascontiguousarray(ins["L0_attn_fir_long"].reshape(1024, 31).T)
    m["firsT"] = np.ascontiguousarray(ins["L0_attn_fir_short"].reshape(1024, 3).T)
    bias20 = np.concatenate([np.zeros(4, np.float32),
                             np.asarray(ins["L0_attn_bg"], np.float32),
                             np.asarray(ins["L0_attn_bl"], np.float32)])
    m["bias20"] = np.tile(bias20, (128, 4)).astype(np.float32)
    m["rms_rep"] = np.tile(np.asarray(ins["L0_attn_rms_w"], np.float32), (128, 4))
    bias_f = np.zeros((128, 8), np.float32)
    for g in range(8):
        bias_f[:, g] = ins["L0_ffn_bias"][128 * g:128 * (g + 1)]
    m["bias_ffn"] = bias_f
    m.update(consts)
    return {k: np.ascontiguousarray(v) for k, v in m.items()}



class Ctr:
    def __init__(self, handle, step=1):
        self.h = handle
        self.step = step
        self.n = 0


SERIAL = False


class Sched:
    def __init__(self):
        self.items = []
        self.last = None

    def stage(self, eng, ctr, waits, emits):
        if SERIAL and self.last is not None:
            waits = list(waits) + [self.last]
        wmap = {}
        for mk in waits:
            if mk is None:
                continue
            c, v = mk
            if c not in wmap or wmap[c] < v:
                wmap[c] = v
        per_instr = ctr.step == 16
        if per_instr and ctr.n > 0:
            if ctr not in wmap or wmap[ctr] < ctr.n:
                wmap[ctr] = ctr.n
        ctr.n += ctr.step * (len(emits) if per_instr else 1)
        target = ctr.n
        wlist = list(wmap.items())

        def fn(e):
            for c, v in wlist:
                if v > 0:
                    e.wait_ge(c.h, v)
            last = None
            for g in emits:
                last = g(e)
                if per_instr:
                    last.then_inc(ctr.h, ctr.step)
            if not per_instr:
                last.then_inc(ctr.h, ctr.step)
        self.items.append((eng, fn))
        self.last = (ctr, target)
        return (ctr, target)



def build_nc():
    nc = bass.Bass("TRN2", target_bir_lowering=False)
    din = {}

    def dram(name, shape, dt=F32, kind="ExternalInput"):
        din[name] = nc.dram_tensor(name, shape, dt, kind=kind).ap()
        return din[name]

    x_h = dram("x", [L, D])
    out_h = dram("out", [L, D], kind="ExternalOutput")
    dram("crow", [1, D])
    dram("ltrow", [1, 4])
    for w in ("wq", "wk", "wv", "wo", "wf_w", "wf_a1", "wf_a2"):
        dram(w, [128, 8 * 1024])
    dram("wbgl", [128, 160])
    dram("wmod", [128, 8 * 6144])
    dram("convw", [128, 96])
    dram("firlT", [31, 1024])
    dram("firsT", [3, 1024])
    dram("bias20", [128, 80])
    dram("rms_rep", [128, 1024])
    dram("bias_ffn", [128, 8])
    dram("eye", [128, 128], BF16)
    dram("masks", [128, 256], BF16)
    dram("onesl2", [128, 32], BF16)
    dram("onehot4", [4, 512])
    dram("ones1", [1, 128])
    dram("ones1b", [1, 128], BF16)
    dram("epsc", [128, 6])
    dram("dft_f", [256, 256], BF16)
    dram("dft_ir", [128, 128], BF16)
    dram("dft_ii", [128, 128], BF16)
    dram("dftw_l", [31, 384], BF16)
    dram("dftw_s", [3, 384], BF16)

    sch = Sched()
    x_r = x_h.rearrange("(n p) f -> p n f", p=128)
    out_r = out_h.rearrange("(n p) f -> p n f", p=128)

    from contextlib import ExitStack
    _stack = ExitStack()
    with _stack:
        x_tm = _stack.enter_context(nc.sbuf_tensor("x_tm", [128, 16 * 1024], F32))
        wo_sb = _stack.enter_context(nc.sbuf_tensor("wo_sb", [128, 8192], BF16))
        S_sb = _stack.enter_context(nc.sbuf_tensor("S_sb", [128, 2048], BF16))
        eye_sb = _stack.enter_context(nc.sbuf_tensor("eye_sb", [128, 128], BF16))
        masks_sb = _stack.enter_context(nc.sbuf_tensor("masks_sb", [128, 256], BF16))
        onesl2_sb = _stack.enter_context(nc.sbuf_tensor("onesl2_sb", [128, 32], BF16))
        onehot4_sb = _stack.enter_context(nc.sbuf_tensor("onehot4_sb", [4, 512], F32))
        ones1_sb = _stack.enter_context(nc.sbuf_tensor("ones1_sb", [1, 128], F32))
        ones1b_sb = _stack.enter_context(nc.sbuf_tensor("ones1b_sb", [1, 128], BF16))
        epsc_sb = _stack.enter_context(nc.sbuf_tensor("epsc_sb", [128, 6], F32))
        bias20_sb = _stack.enter_context(nc.sbuf_tensor("bias20_sb", [128, 80], F32))
        rms_rep_sb = _stack.enter_context(nc.sbuf_tensor("rms_rep_sb", [128, 1024], F32))
        bias_ffn_sb = _stack.enter_context(nc.sbuf_tensor("bias_ffn_sb", [128, 8], F32))
        convw_sb = _stack.enter_context(nc.sbuf_tensor("convw_sb", [128, 96], F32))
        gates_fm = _stack.enter_context(nc.sbuf_tensor("gates_fm", [128, 48], F32))
        invtemp_rep = _stack.enter_context(nc.sbuf_tensor("invtemp_rep", [128, 4], F32))
        dftf_sb = _stack.enter_context(nc.sbuf_tensor("dftf_sb", [128, 512], BF16))
        dftir_sb = _stack.enter_context(nc.sbuf_tensor("dftir_sb", [128, 128], BF16))
        dftii_sb = _stack.enter_context(nc.sbuf_tensor("dftii_sb", [128, 128], BF16))
        whl_sb = _stack.enter_context(nc.sbuf_tensor("whl_sb", [128, 3072], BF16))
        whs_sb = _stack.enter_context(nc.sbuf_tensor("whs_sb", [128, 3072], BF16))
        cfm_sb = _stack.enter_context(nc.sbuf_tensor("cfm_sb", [128, 8], BF16))
        ltrow_sb = _stack.enter_context(nc.sbuf_tensor("ltrow_sb", [1, 4], F32))
        itrow_sb = _stack.enter_context(nc.sbuf_tensor("itrow_sb", [1, 4], F32))
        h_blk = _stack.enter_context(nc.sbuf_tensor("h_blk", [128, 4096], BF16))
        qh_blk = _stack.enter_context(nc.sbuf_tensor("qh_blk", [128, 4096], BF16))
        kh_blk = _stack.enter_context(nc.sbuf_tensor("kh_blk", [128, 4096], BF16))
        v_blk = _stack.enter_context(nc.sbuf_tensor("v_blk", [128, 4096], BF16))
        z_blk = _stack.enter_context(nc.sbuf_tensor("z_blk", [128, 8 * 516], BF16))
        wfm = z_blk[:, 0:1024]
        attnT = z_blk[:, 1024:1536]
        y_blk = _stack.enter_context(nc.sbuf_tensor("y_blk", [128, 4096], BF16))
        zh_q = _stack.enter_context(nc.sbuf_tensor("zh_q", [128, 32], BF16))
        zh_k = _stack.enter_context(nc.sbuf_tensor("zh_k", [128, 32], BF16))
        zh_v = _stack.enter_context(nc.sbuf_tensor("zh_v", [128, 32], BF16))
        w_str = _stack.enter_context(nc.sbuf_tensor("w_str", [128, 2048], BF16))
        wbgl_sb = _stack.enter_context(nc.sbuf_tensor("wbgl_sb", [128, 160], BF16))
        bgl_tm = _stack.enter_context(nc.sbuf_tensor("bgl_tm", [128, 80], F32))
        beta_blk = _stack.enter_context(nc.sbuf_tensor("beta_blk", [128, 16], F32))
        wg_t = _stack.enter_context(nc.sbuf_tensor("wg_t", [128, 16], F32))
        mix_b2 = _stack.enter_context(nc.sbuf_tensor("mix_b2", [128, 1024], F32))
        sq_scr = _stack.enter_context(nc.sbuf_tensor("sq_scr", [128, 256], F32))
        sm_m = _stack.enter_context(nc.sbuf_tensor("sm_m", [128, 16], F32))
        sm_e = _stack.enter_context(nc.sbuf_tensor("sm_e", [128, 48], F32))
        sm_s = _stack.enter_context(nc.sbuf_tensor("sm_s", [128, 16], F32))
        w1m = _stack.enter_context(nc.sbuf_tensor("w1m", [128, 16], F32))
        rs4 = _stack.enter_context(nc.sbuf_tensor("rs4", [4, 512], F32))
        rs_rep = _stack.enter_context(nc.sbuf_tensor("rs_rep", [128, 1024], BF16))
        dtiles = _stack.enter_context(nc.sbuf_tensor("dtiles", [128, 6144], BF16))
        ktm = dtiles[:, 0:1024]
        kbtm = dtiles[:, 1024:2048]
        vtm = dtiles[:, 2048:3072]
        vbtm = dtiles[:, 3072:4096]
        vtm_prev = dtiles[:, 4096:5120]
        u2_tm = dtiles[:, 5120:6144]

        Pp = _stack.enter_context(nc.sbuf_tensor("Pp", [128, 1024], BF16))
        Qp = _stack.enter_context(nc.sbuf_tensor("Qp", [128, 1024], BF16))
        Rt = _stack.enter_context(nc.sbuf_tensor("Rt", [128, 1024], BF16))
        xhat = _stack.enter_context(nc.sbuf_tensor("xhat", [128, 2048], BF16))
        yhat = _stack.enter_context(nc.sbuf_tensor("yhat", [128, 2048], BF16))
        ls_tm = _stack.enter_context(nc.sbuf_tensor("ls_tm", [128, 1024], BF16))
        ss_tm = _stack.enter_context(nc.sbuf_tensor("ss_tm", [128, 1024], BF16))
        ofm_sb = _stack.enter_context(nc.sbuf_tensor("ofm_sb", [128, 1024], BF16))
        mix_s = _stack.enter_context(nc.sbuf_tensor("mix_s", [128, 1024], F32))
        rsums = _stack.enter_context(nc.sbuf_tensor("rsums", [128, 4], F32))
        rstd2 = _stack.enter_context(nc.sbuf_tensor("rstd2", [128, 4], F32))
        o_rms = _stack.enter_context(nc.sbuf_tensor("o_rms", [128, 1024], BF16))
        st6 = _stack.enter_context(nc.sbuf_tensor("st6", [128, 48], F32))
        mv = _stack.enter_context(nc.sbuf_tensor("mv", [128, 8], F32))
        rstd_ln = _stack.enter_context(nc.sbuf_tensor("rstd_ln", [128, 4], F32))
        nmu_ln = _stack.enter_context(nc.sbuf_tensor("nmu_ln", [128, 4], F32))
        mod_sb = dtiles[0:1, :]
        ga_rep = ls_tm[:]
        bglf_sb = rs_rep[0:20, 0:512]
        wmod_p0 = z_blk[:, 0:1536]
        wmod_p1 = z_blk[:, 1536:3072]
        ps_tr = _stack.enter_context(nc.psum_tensor("ps_tr", [128, 1024], BF16))
        ps_g = _stack.enter_context(nc.psum_tensor("ps_g", [128, 512], F32))
        ps_p = _stack.enter_context(nc.psum_tensor("ps_p", [128, 512], F32))
        ps_q = _stack.enter_context(nc.psum_tensor("ps_q", [128, 512], F32))
        ps_u = _stack.enter_context(nc.psum_tensor("ps_u", [128, 1024], F32))
        ps_o = _stack.enter_context(nc.psum_tensor("ps_o", [128, 1024], F32))
        _pe = _stack.enter_context(nc.semaphore("s_pe"))
        _dve = _stack.enter_context(nc.semaphore("s_dve"))
        _act = _stack.enter_context(nc.semaphore("s_act"))
        _ld = _stack.enter_context(nc.semaphore("s_ld"))
        _ldx = _stack.enter_context(nc.semaphore("s_ldx"))
        _gld1 = _stack.enter_context(nc.semaphore("s_gld1"))
        _gld = _stack.enter_context(nc.semaphore("s_gld"))
        _st = _stack.enter_context(nc.semaphore("s_st"))
        block = _stack.enter_context(nc.Block())
        PE = Ctr(_pe); DVE = Ctr(_dve); ACT = Ctr(_act)
        LD = Ctr(_ld, 16); GLD = Ctr(_gld, 16); ST = Ctr(_st, 16)
        LDX = Ctr(_ldx, 16); GLD1 = Ctr(_gld1, 16)
        st = sch.stage

        m_consts = st("sync", LD, [], [
            lambda e: e.dma_start(eye_sb[:], din["eye"][:, :]),
            lambda e: e.dma_start(masks_sb[:], din["masks"][:, :]),
            lambda e: e.dma_start(onesl2_sb[:], din["onesl2"][:, :]),
            lambda e: e.dma_start(onehot4_sb[:], din["onehot4"][:, :]),
            lambda e: e.dma_start(ones1_sb[:], din["ones1"][:, :]),
            lambda e: e.dma_start(ones1b_sb[:], din["ones1b"][:, :]),
            lambda e: e.dma_start(epsc_sb[:], din["epsc"][:, :]),
            lambda e: e.dma_start(bias20_sb[:], din["bias20"][:, :]),
            lambda e: e.dma_start(rms_rep_sb[:], din["rms_rep"][:, :]),
            lambda e: e.dma_start(bias_ffn_sb[:], din["bias_ffn"][:, :]),
            lambda e: e.dma_start(convw_sb[:], din["convw"][:, :]),
            lambda e: e.dma_start(dftf_sb[:].rearrange("p (k b) -> p k b", k=2), din["dft_f"].rearrange("(k p) b -> p k b", p=128)),
            lambda e: e.dma_start(dftir_sb[:], din["dft_ir"][:, :]),
            lambda e: e.dma_start(dftii_sb[:], din["dft_ii"][:, :]),
            lambda e: e.dma_start(yhat[0:1, :].bitcast(F32), din["crow"][:, :]),
            lambda e: e.dma_start(ltrow_sb[:], din["ltrow"][:, :]),
            lambda e: e.dma_start(xhat[0:31, 0:384], din["dftw_l"][:, :]),
            lambda e: e.dma_start(xhat[0:3, 384:768], din["dftw_s"][:, :]),
        ])
        m_x = st("sync", LDX, [], [lambda e: e.dma_start(x_tm[:].rearrange("p (n f) -> p n f", n=16), x_r[:, :, :])])
        m_wo = st("gpsimd", GLD, [], [lambda e: e.dma_start(wo_sb[:], din["wo"][:, :])])
        m_wbgl = st("gpsimd", GLD, [], [lambda e: e.dma_start(wbgl_sb[:], din["wbgl"][:, :])])
        m_firT = st("gpsimd", GLD, [], [
            lambda e: e.dma_start(v_blk[0:31, 0:1024], din["firlT"][:, :]),
            lambda e: e.dma_start(v_blk[0:3, 1024:2048], din["firsT"][:, :]),
        ])

        wh_last = None
        for whbuf, taps, dofs, frofs in ((whl_sb, 31, 0, 0), (whs_sb, 3, 384, 1024)):
            for j in range(3):
                mm = st("tensor", PE, [m_consts, m_firT, wh_last], [
                    (lambda e, t=taps, d=dofs, fr=frofs, j=j, ns=ns: e.matmul(
                        ps_u[:, 512 * ns:512 * (ns + 1)],
                        xhat[0:t, d + 128 * j: d + 128 * (j + 1)],
                        v_blk[0:t, fr + 512 * ns: fr + 512 * (ns + 1)],
                        start=True, stop=True))
                    for ns in range(2)
                ])
                wh_last = st("vector", DVE, [mm], [
                    (lambda e, w=whbuf, j=j: e.tensor_copy(
                        w[:, 1024 * j:1024 * (j + 1)], ps_u[:, 0:1024]))
                ])
        m_wh = wh_last

        m_csilu = st("scalar", ACT, [m_consts], [
            lambda e: e.activation(ls_tm[0:1, 0:1024], yhat[0:1, :].bitcast(F32), AF.Silu)])
        m_it = st("scalar", ACT, [m_consts], [
            lambda e: e.activation(itrow_sb[:], ltrow_sb[:], AF.Exp, scale=epsc_sb[0:1, 3:4])])
        cfm_mm = st("tensor", PE, [m_csilu, m_it], [
            (lambda e, g=g: e.matmul(ps_o[:, g:g + 1], ls_tm[0:1, 128 * g:128 * (g + 1)],
                                     ones1b_sb[0:1, 0:1], start=True, stop=True))
            for g in range(8)
        ] + [lambda e: e.matmul(ps_o[:, 8:12], ones1_sb[0:1, :], itrow_sb[0:1, :],
                                start=True, stop=True)])
        m_cfm = st("vector", DVE, [cfm_mm], [
            lambda e: e.tensor_copy(cfm_sb[:], ps_o[:, 0:8]),
            lambda e: e.tensor_copy(invtemp_rep[:], ps_o[:, 8:12]),
        ])

        ps_slices = [ps_g[0:1, 0:512], ps_p[0:1, 0:512], ps_q[0:1, 0:512]]
        mod_ev_prev = None
        wm_mms = []
        for q in range(4):
            for g in range(8):
                pp = (wmod_p0, wmod_p1)[g % 2]
                waits = [m_cfm, m_wh]
                if len(wm_mms) >= 2:
                    waits.append(wm_mms[-2])
                if g == 0 and mod_ev_prev is not None:
                    waits.append(mod_ev_prev)
                dm = st("gpsimd", GLD, waits, [
                    (lambda e, pp=pp, g=g, q=q: e.dma_start(
                        pp, din["wmod"][:, 6144 * g + 1536 * q: 6144 * g + 1536 * (q + 1)]))
                ])
                mmw = [dm]
                if g == 0 and mod_ev_prev is not None:
                    mmw.append(mod_ev_prev)
                mm = st("tensor", PE, mmw, [
                    (lambda e, pp=pp, g=g, sl=sl: e.matmul(
                        ps_slices[sl], cfm_sb[:, g:g + 1], pp[:, 512 * sl:512 * (sl + 1)],
                        start=(g == 0), stop=(g == 7)))
                    for sl in range(3)
                ])
                wm_mms.append(mm)
            mod_ev_prev = st("vector", DVE, [wm_mms[-1]], [
                (lambda e, sl=sl, q=q: e.tensor_copy(
                    mod_sb[0:1, 1536 * q + 512 * sl: 1536 * q + 512 * (sl + 1)], ps_slices[sl]))
                for sl in range(3)
            ])
        m_mod = mod_ev_prev

        ga_row = mod_sb[0:1, 2048:3072]
        absrow = mix_s[0:1, :].bitcast(BF16)[0:1, 0:1024]
        sgnrow = yhat[0:1, 0:1024]
        zcrow = yhat[0:1, 1024:2048]
        m_abs = st("scalar", ACT, [m_mod, m_wh], [
            lambda e: e.activation(absrow, ga_row, AF.Abs),
            lambda e: e.activation(sgnrow, ga_row, AF.Sign),
        ])
        m_cl1 = st("vector", DVE, [m_abs], [
            lambda e: e.tensor_scalar_max(absrow, absrow, 0.25),
            lambda e: e.tensor_tensor(zcrow, sgnrow, sgnrow, ALU.mult),
        ])
        m_cl2 = st("vector", DVE, [m_cl1], [
            lambda e: e.tensor_scalar(zcrow, zcrow, -1.0, 1.0, ALU.mult, ALU.add)])
        m_cl3 = st("vector", DVE, [m_cl2], [
            lambda e: e.tensor_add(sgnrow, sgnrow, zcrow)])
        m_clamp = st("vector", DVE, [m_cl3], [
            lambda e: e.tensor_tensor(ga_row, absrow, sgnrow, ALU.mult)])
        gt_mm = st("tensor", PE, [m_clamp], [
            (lambda e, v=v, g=g: e.matmul(
                ps_o[:, 8 * v + g: 8 * v + g + 1],
                mod_sb[0:1, 1024 * v + 128 * g: 1024 * v + 128 * (g + 1)],
                ones1b_sb[0:1, 0:1], start=True, stop=True))
            for v in range(6) for g in range(8)
        ])
        m_gates0 = st("vector", DVE, [gt_mm], [
            lambda e: e.tensor_copy(gates_fm[:], ps_o[:, 0:48])])
        m_gbabs = st("scalar", ACT, [m_gates0], [
            lambda e: e.activation(st6[:, 0:8], gates_fm[:, 40:48], AF.Abs),
            lambda e: e.activation(st6[:, 8:16], gates_fm[:, 40:48], AF.Sign)])
        m_g1 = st("vector", DVE, [m_gbabs], [
            lambda e: e.tensor_scalar_add(gates_fm[:, 8:16], gates_fm[:, 8:16], 1.0),
            lambda e: e.tensor_scalar_add(gates_fm[:, 32:40], gates_fm[:, 32:40], 1.0),
            lambda e: e.tensor_scalar_max(st6[:, 0:8], st6[:, 0:8], 0.25),
            lambda e: e.tensor_tensor(st6[:, 16:24], st6[:, 8:16], st6[:, 8:16], ALU.mult),
        ])
        m_g2 = st("vector", DVE, [m_g1], [
            lambda e: e.tensor_scalar(st6[:, 16:24], st6[:, 16:24], -1.0, 1.0, ALU.mult, ALU.add)])
        m_g3 = st("vector", DVE, [m_g2], [
            lambda e: e.tensor_add(st6[:, 8:16], st6[:, 8:16], st6[:, 16:24])])
        m_gates = st("vector", DVE, [m_g3], [
            lambda e: e.tensor_tensor(gates_fm[:, 40:48], st6[:, 0:8], st6[:, 8:16], ALU.mult)])
        gar_mm = st("tensor", PE, [m_clamp, m_gates], [
            lambda e: e.matmul(ps_o[:, 0:512], ones1b_sb[0:1, :], mod_sb[0:1, 2048:2560],
                               start=True, stop=True),
            lambda e: e.matmul(ps_o[:, 512:1024], ones1b_sb[0:1, :], mod_sb[0:1, 2560:3072],
                               start=True, stop=True),
        ])
        m_gar = st("vector", DVE, [gar_mm], [
            lambda e: e.tensor_copy(ga_rep, ps_o[:, 0:1024])])
        m_wofold = st("vector", DVE, [m_gar, m_wo], [
            lambda e: e.tensor_tensor(
                wo_sb[:].rearrange("p (g f) -> p g f", g=8),
                wo_sb[:].rearrange("p (g f) -> p g f", g=8),
                ga_rep[:].unsqueeze(1).broadcast_to([128, 8, 1024]),
                ALU.mult)])
        m_hist0 = st("vector", DVE, [gt_mm, gar_mm], [
            lambda e: e.memset(zh_q[:], 0.0),
            lambda e: e.memset(zh_k[:], 0.0),
            lambda e: e.memset(zh_v[:], 0.0),
            lambda e: e.memset(S_sb[:], 0.0),
            lambda e: e.memset(vtm_prev, 0.0),
        ])

        state = {
            "m_h": m_consts, "m_x": m_x,
            "wstr_free": m_consts,
            "psg_free": m_gates, "pstr_free": m_wofold,
            "pspq_free": [m_mod, m_mod],
            "psu_free": m_wh, "pso_free": m_gar, "ofm_free": m_hist0,
            "m_S": m_hist0, "m_vtmprev": m_hist0, "m_mix_done": m_hist0,
            "m_clamp": m_clamp,
            "ln_free": m_gates,
            "rsrep_free": m_gates,
            "y_free": m_hist0,
            "z_free": m_hist0,
        }
        ln_gate_cols = [(1, 0), (4, 3)]

        def ln_and_h(b, which):
            cvec, svec = ln_gate_cols[which]
            xofs = 4096 * b
            stats0 = st("vector", DVE, [state["m_x"], state["ln_free"]], [
                (lambda e, n=n, hf=hf: e.bn_stats(
                    st6[:, 12 * n + 6 * hf: 12 * n + 6 * hf + 6],
                    x_tm[:, xofs + 1024 * n + 512 * hf: xofs + 1024 * n + 512 * (hf + 1)]))
                for n in range(4) for hf in range(2)
            ])
            stats = st("vector", DVE, [stats0], [
                (lambda e, n=n: e.bn_aggr(mv[:, 2 * n:2 * n + 2], st6[:, 12 * n:12 * n + 12]))
                for n in range(4)
            ])
            sq0 = st("scalar", ACT, [stats], [
                lambda e: e.activation(rstd_ln[:], mv[:, 1:8:2], AF.Ln, bias=epsc_sb[:, 0:1])])
            sq = st("scalar", ACT, [sq0], [
                lambda e: e.activation(rstd_ln[:], rstd_ln[:], AF.Exp, scale=epsc_sb[:, 4:5])])
            rst = st("vector", DVE, [sq], [
                lambda e: e.scalar_tensor_tensor(nmu_ln[:], mv[:, 0:8:2], -1.0, rstd_ln[:],
                                                 ALU.mult, ALU.mult),
            ])
            apply = st("scalar", ACT, [rst, state["y_free"]], [
                (lambda e, n=n: e.activation(
                    y_blk[:, 1024 * n:1024 * (n + 1)],
                    x_tm[:, xofs + 1024 * n: xofs + 1024 * (n + 1)],
                    AF.Identity, bias=nmu_ln[:, n:n + 1], scale=rstd_ln[:, n:n + 1]))
                for n in range(4)
            ])
            m_prev = state["pstr_free"]
            last_tmm = None
            for gp in range(4):
                tmm = st("tensor", PE, [apply, m_prev], [
                    (lambda e, g=g, n=n: e.transpose(
                        ps_tr[:, 512 * (g % 2) + 128 * n: 512 * (g % 2) + 128 * (n + 1)],
                        y_blk[:, 1024 * n + 128 * g: 1024 * n + 128 * g + 128], eye_sb[:]))
                    for g in (2 * gp, 2 * gp + 1) for n in range(4)
                ])
                m_prev = st("scalar", ACT, [tmm], [
                    (lambda e, g=g: e.activation(
                        h_blk[:, 512 * g:512 * (g + 1)],
                        ps_tr[:, 512 * (g % 2):512 * (g % 2) + 512],
                        AF.Identity, bias=gates_fm[:, 8 * svec + g: 8 * svec + g + 1],
                        scale=gates_fm[:, 8 * cvec + g: 8 * cvec + g + 1]))
                    for g in (2 * gp, 2 * gp + 1)
                ])
                last_tmm = tmm
            state["pstr_free"] = m_prev
            state["m_h"] = m_prev
            state["y_free"] = last_tmm
            state["ln_free"] = apply

        def project(b, wname, dst, zhist, cwofs):
            evs = []
            halves = [w_str[:, 0:1024], w_str[:, 1024:2048]]
            hfree = [state["wstr_free"], state["wstr_free"]]
            for qq in range(4):
                dmas = []
                for hb in range(2):
                    dmas.append(st("gpsimd", GLD, [hfree[hb]], [
                        (lambda e, wname=wname, qq=qq, hb=hb: e.dma_start(
                            halves[hb], din[wname][:, 2048 * qq + 1024 * hb:
                                                   2048 * qq + 1024 * (hb + 1)]))]))
                mm = None
                for hb in range(2):
                    for mh in range(2):
                        mc = 2 * qq + mh
                        pp = (ps_p, ps_q)[mh]
                        mm = st("tensor", PE, [dmas[hb], state["m_h"],
                                               state["pspq_free"][mh]], [
                            (lambda e, g=g, mh=mh, hb=hb, pp=pp: e.matmul(
                                pp[:, 0:512],
                                halves[hb][:, 256 * (g - 4 * hb) + 128 * mh:
                                           256 * (g - 4 * hb) + 128 * mh + 128],
                                h_blk[:, 512 * g:512 * (g + 1)],
                                start=(g == 0), stop=(g == 7)))
                            for g in range(4 * hb, 4 * hb + 4)
                        ])
                        if mh == 1:
                            hfree[hb] = mm
                for mh in range(2):
                    mc = 2 * qq + mh
                    pp = (ps_p, ps_q)[mh]
                    ev = st("scalar", ACT, [mm, state["z_free"]], [
                        (lambda e, m=mc, pp=pp: e.copy(
                            z_blk[:, 516 * m + 4: 516 * m + 516], pp[:, 0:512]))
                    ])
                    evs.append(ev)
                    state["pspq_free"][mh] = ev
            state["wstr_free"] = mm
            zv = z_blk[:].rearrange("p (g t) -> p g t", g=8)
            zhv = zhist[:].rearrange("p (g t) -> p g t", g=8)
            c_hi = st("vector", DVE, evs + [m_hist0, state["y_free"]], [
                (lambda e: e.tensor_max(zv[:, :, 0:4], zhv, zhv)),
            ])
            c_ts = st("vector", DVE, [c_hi], [
                (lambda e, g=g: e.tensor_scalar(
                    y_blk[:, 512 * g:512 * (g + 1)], z_blk[:, 516 * g + 4: 516 * g + 516],
                    convw_sb[:, cwofs + 4 * g + 3: cwofs + 4 * g + 4], None, ALU.mult))
                for g in range(8)
            ])
            conv = c_ts
            for j in range(3):
                conv = st("vector", DVE, [conv], [
                    (lambda e, g=g, j=j: e.scalar_tensor_tensor(
                        y_blk[:, 512 * g:512 * (g + 1)],
                        z_blk[:, 516 * g + j + 1: 516 * g + j + 513],
                        convw_sb[:, cwofs + 4 * g + j: cwofs + 4 * g + j + 1],
                        y_blk[:, 512 * g:512 * (g + 1)], ALU.mult, ALU.add))
                    for g in range(8)
                ])
            c_ho = st("vector", DVE, [conv], [
                (lambda e: e.tensor_max(zhv, zv[:, :, 512:516], zv[:, :, 512:516])),
            ])
            state["z_free"] = c_ho
            sl = st("scalar", ACT, [conv], [
                lambda e: e.activation(dst[:], y_blk[:], AF.Silu)])
            state["y_free"] = sl
            return sl

        def l2norm(src, m_src):
            sq = st("vector", DVE, [m_src, state["y_free"]], [
                lambda e: e.tensor_tensor(y_blk[:], src[:], src[:], ALU.mult)])
            smm = st("tensor", PE, [sq, state["psg_free"]], [
                (lambda e, g=g: e.matmul(ps_g[0:4, 0:512], onesl2_sb[:, 4 * g:4 * g + 4],
                                         y_blk[:, 512 * g:512 * (g + 1)],
                                         start=(g == 0), stop=(g == 7)))
                for g in range(8)
            ])
            state["y_free"] = smm
            rt0 = st("scalar", ACT, [smm], [
                lambda e: e.activation(rs4[0:4, 0:512], ps_g[0:4, 0:512],
                                       AF.Ln, bias=epsc_sb[0:4, 1:2])])
            rt = st("scalar", ACT, [rt0], [
                lambda e: e.activation(rs4[0:4, 0:512], rs4[0:4, 0:512],
                                       AF.Exp, scale=epsc_sb[0:4, 4:5])])
            state["psg_free"] = rt
            rc = rt
            marks = []
            for h in range(4):
                half = h % 2
                waits = [rc, state["pso_free"]]
                if len(marks) >= 2:
                    waits.append(marks[-2])
                rep = st("tensor", PE, waits, [
                    (lambda e, h=h, half=half: e.matmul(
                        ps_o[:, 512 * half:512 * (half + 1)],
                        onehot4_sb[:, 128 * h:128 * (h + 1)], rs4[0:4, 0:512],
                        start=True, stop=True))
                ])
                evr = st("scalar", ACT, [rep], [
                    (lambda e, half=half: e.copy(rs_rep[:, 512 * half:512 * (half + 1)],
                                                 ps_o[:, 512 * half:512 * (half + 1)]))
                ])
                ap = st("vector", DVE, [evr], [
                    (lambda e, h=h, half=half: e.tensor_tensor(
                        src[:, 1024 * h:1024 * (h + 1)].rearrange("p (c t) -> p c t", c=2),
                        src[:, 1024 * h:1024 * (h + 1)].rearrange("p (c t) -> p c t", c=2),
                        rs_rep[:, 512 * half:512 * (half + 1)].unsqueeze(1)
                        .broadcast_to([128, 2, 512]),
                        ALU.mult))
                ])
                marks.append(ap)
            state["pso_free"] = marks[-1]
            return marks[-1]

        def bgl_stage(m_after):
            mm = st("tensor", PE, [state["m_h"], m_wbgl, state["psg_free"]], [
                (lambda e, g=g: e.matmul(ps_g[0:20, 0:512], wbgl_sb[:, 20 * g:20 * (g + 1)],
                                         h_blk[:, 512 * g:512 * (g + 1)],
                                         start=(g == 0), stop=(g == 7)))
                for g in range(8)
            ])
            ev = st("scalar", ACT, [mm, m_after], [
                lambda e: e.copy(bglf_sb, ps_g[0:20, 0:512])])
            state["psg_free"] = ev
            tmm = st("tensor", PE, [ev, state["pstr_free"]], [
                (lambda e, tc=tc: e.transpose(ps_tr[:, 20 * tc:20 * (tc + 1)],
                                              bglf_sb[0:20, 128 * tc:128 * (tc + 1)],
                                              eye_sb[0:20, 0:20]))
                for tc in range(4)
            ])
            evb = st("vector", DVE, [tmm], [
                lambda e: e.tensor_add(bgl_tm[:], ps_tr[:, 0:80], bias20_sb[:])])
            state["pstr_free"] = evb
            bv = bgl_tm[:].rearrange("p (tc k) -> p tc k", tc=4)
            smev = sm_e[:].rearrange("p (tc h j) -> p tc h j", tc=4, h=4)
            lv = bv[:, :, 8:20].rearrange("p tc (h j) -> p tc h j", h=4)
            sig = st("scalar", ACT, [evb], [
                lambda e: e.activation(beta_blk[:].rearrange("p (tc h) -> p tc h", tc=4),
                                       bv[:, :, 0:4], AF.Sigmoid),
                lambda e: e.activation(wg_t[:].rearrange("p (tc h) -> p tc h", tc=4),
                                       bv[:, :, 4:8], AF.Sigmoid),
            ])
            smax0 = st("vector", DVE, [sig], [
                lambda e: e.tensor_reduce(sm_m[:].rearrange("p (tc h) -> p tc h", tc=4),
                                          lv, AXL.X, ALU.max)])
            smax1 = st("vector", DVE, [smax0], [
                lambda e: e.tensor_tensor(smev, lv,
                                          sm_m[:].rearrange("p (tc h) -> p tc h", tc=4)
                                          .unsqueeze(3).broadcast_to([128, 4, 4, 3]),
                                          ALU.subtract)])
            smax = st("vector", DVE, [smax1], [
                lambda e: e.tensor_tensor(smev, smev,
                                          invtemp_rep[:].unsqueeze(1).unsqueeze(3)
                                          .broadcast_to([128, 4, 4, 3]),
                                          ALU.mult)])
            ex = st("scalar", ACT, [smax], [
                lambda e: e.activation(sm_e[:], sm_e[:], AF.Exp)])
            red = st("vector", DVE, [ex], [
                lambda e: e.tensor_reduce(sm_s[:].rearrange("p (tc h) -> p tc h", tc=4),
                                          smev, AXL.X, ALU.add),
                lambda e: e.tensor_scalar(w1m[:], wg_t[:], -1.0, 1.0, ALU.mult, ALU.add),
            ])
            rcp0 = st("scalar", ACT, [red], [
                lambda e: e.activation(sm_m[:], sm_s[:], AF.Ln)])
            rcp = st("scalar", ACT, [rcp0], [
                lambda e: e.activation(sm_m[:], sm_m[:], AF.Exp, scale=epsc_sb[:, 3:4])])
            fin0 = st("vector", DVE, [rcp], [
                lambda e: e.tensor_tensor(smev, smev,
                                          w1m[:].rearrange("p (tc h) -> p tc h", tc=4)
                                          .unsqueeze(3).broadcast_to([128, 4, 4, 3]), ALU.mult)])
            fin = st("vector", DVE, [fin0], [
                lambda e: e.tensor_tensor(smev, smev,
                                          sm_m[:].rearrange("p (tc h) -> p tc h", tc=4)
                                          .unsqueeze(3).broadcast_to([128, 4, 4, 3]), ALU.mult)])
            return fin

        state["pend_tail"] = None

        def emit_tail():
            td = state["pend_tail"]
            state["pend_tail"] = None
            ot = st("tensor", PE, [td["orms"], state["pstr_free"]], [
                (lambda e, j=j: e.transpose(ps_tr[:, 128 * j:128 * (j + 1)],
                                            o_rms[:, 128 * j:128 * (j + 1)], eye_sb[:]))
                for j in range(8)
            ])
            oev = st("scalar", ACT, [ot, state["ofm_free"]], [
                lambda e: e.copy(ofm_sb[:], ps_tr[:, 0:1024])])
            state["pstr_free"] = oev
            wo_mm = st("tensor", PE, [oev, m_wofold, state["pso_free"]], [
                (lambda e, g=g, sl=sl: e.matmul(
                    ps_o[:, 512 * sl:512 * (sl + 1)], ofm_sb[:, 128 * g:128 * (g + 1)],
                    wo_sb[:, 1024 * g + 512 * sl: 1024 * g + 512 * (sl + 1)],
                    start=(g == 0), stop=(g == 7)))
                for sl in range(2) for g in range(8)
            ])
            state["ofm_free"] = wo_mm
            res = st("vector", DVE, [wo_mm], [
                (lambda e, b=td["b"], n=td["n"]: e.tensor_add(
                    x_tm[:, 4096 * b + 1024 * n: 4096 * b + 1024 * (n + 1)],
                    x_tm[:, 4096 * b + 1024 * n: 4096 * b + 1024 * (n + 1)],
                    ps_o[:, 0:1024]))
            ])
            state["pso_free"] = res
            state["m_x"] = res

        for b in range(NB):
            ln_and_h(b, 0)
            m_q = project(b, "wq", qh_blk, zh_q, 0)
            m_bgl = bgl_stage(state["rsrep_free"])
            m_k = project(b, "wk", kh_blk, zh_k, 32)
            m_v = project(b, "wv", v_blk, zh_v, 64)
            m_ql2 = l2norm(qh_blk, m_q)
            m_kl2 = l2norm(kh_blk, m_k)
            state["rsrep_free"] = m_kl2

            for n in range(NCB):
                cofs = 128 * n
                bc = 16 * n
                bb = 4 * n
                t1 = st("tensor", PE, [m_kl2, state["pstr_free"]], [
                    (lambda e, h=h, dc=dc, cofs=cofs: e.transpose(
                        ps_tr[:, 256 * h + 128 * dc: 256 * h + 128 * (dc + 1)],
                        kh_blk[:, 512 * (2 * h + dc) + cofs: 512 * (2 * h + dc) + cofs + 128],
                        eye_sb[:]))
                    for h in range(4) for dc in range(2)
                ])
                e1a = st("scalar", ACT, [t1], [
                    lambda e: e.copy(ktm, ps_tr[:, 0:1024])])
                e1 = st("vector", DVE, [t1, e1a, m_bgl], [
                    (lambda e, bb=bb: e.tensor_tensor(
                        kbtm.rearrange("p (h t) -> p h t", h=4),
                        ps_tr[:, 0:1024].rearrange("p (h t) -> p h t", h=4),
                        beta_blk[:, bb:bb + 4].unsqueeze(2).broadcast_to([128, 4, 256]),
                        ALU.mult)),
                ])
                state["pstr_free"] = e1
                t2 = st("tensor", PE, [m_v, e1, state["m_vtmprev"]], [
                    (lambda e, h=h, dc=dc, cofs=cofs: e.transpose(
                        ps_tr[:, 256 * h + 128 * dc: 256 * h + 128 * (dc + 1)],
                        v_blk[:, 512 * (2 * h + dc) + cofs: 512 * (2 * h + dc) + cofs + 128],
                        eye_sb[:]))
                    for h in range(4) for dc in range(2)
                ])
                e2a = st("scalar", ACT, [t2], [
                    lambda e: e.copy(vtm, ps_tr[:, 0:1024])])
                e2 = st("vector", DVE, [t2, e2a, m_bgl], [
                    (lambda e, bb=bb: e.tensor_tensor(
                        vbtm.rearrange("p (h t) -> p h t", h=4),
                        ps_tr[:, 0:1024].rearrange("p (h t) -> p h t", h=4),
                        beta_blk[:, bb:bb + 4].unsqueeze(2).broadcast_to([128, 4, 256]),
                        ALU.mult)),
                ])
                state["pstr_free"] = e2

                ffw = st("tensor", PE, [e2a, state["m_vtmprev"], state["psu_free"],
                                        state["psg_free"], state["pspq_free"][0]], [
                    (lambda e, kc=kc, ns=ns: e.matmul(
                        ps_u[:, 512 * ns:512 * (ns + 1)],
                        dftf_sb[:, 256 * kc: 256 * kc + 128],
                        (vtm_prev if kc == 0 else vtm)[:, 512 * ns:512 * (ns + 1)],
                        start=(kc == 0), stop=(kc == 1)))
                    for ns in range(2) for kc in range(2)
                ] + [
                    (lambda e, kc=kc, ns=ns: e.matmul(
                        (ps_g if ns == 0 else ps_p)[:, 0:512],
                        dftf_sb[:, 256 * kc + 128: 256 * kc + 256],
                        (vtm_prev if kc == 0 else vtm)[:, 512 * ns:512 * (ns + 1)],
                        start=(kc == 0), stop=(kc == 1)))
                    for ns in range(2) for kc in range(2)
                ])
                exh = st("scalar", ACT, [ffw], [
                    lambda e: e.copy(xhat[:, 0:1024], ps_u[:, 0:1024]),
                    lambda e: e.copy(xhat[:, 1024:1536], ps_g[:, 0:512]),
                    lambda e: e.copy(xhat[:, 1536:2048], ps_p[:, 0:512]),
                ])
                state["psu_free"] = exh
                state["psg_free"] = exh
                state["pspq_free"][0] = exh
                state["m_vtmprev"] = st("vector", DVE, [e2a, exh], [
                    lambda e: e.tensor_copy(vtm_prev, vtm)])
                mscr = mix_s[:].bitcast(BF16)
                yhat2 = mix_b2[:].bitcast(BF16)
                _yh = (yhat[:], yhat2)
                _whf = (whl_sb, whs_sb)
                _sc = (mscr[:, 0:1024], mscr[:, 1024:2048])
                cmw = [exh, m_wh, state["m_mix_done"], state["m_clamp"],
                       state["psu_free"]]
                cmf = [None, None]
                for f in range(2):
                    c1 = st("vector", DVE, (cmw if f == 0 else [cmf[0]]), [
                        (lambda e, f=f: e.tensor_tensor(
                            _yh[f][:, 0:1024], xhat[:, 0:1024], _whf[f][:, 0:1024], ALU.mult)),
                        (lambda e, f=f: e.tensor_tensor(
                            _sc[f], xhat[:, 1024:2048], _whf[f][:, 1024:2048], ALU.mult)),
                        (lambda e, f=f: e.tensor_tensor(
                            _yh[f][:, 1024:2048], xhat[:, 0:1024], _whf[f][:, 1024:2048], ALU.mult)),
                    ])
                    c2 = st("vector", DVE, [c1], [
                        (lambda e, f=f: e.tensor_sub(
                            _yh[f][:, 0:1024], _yh[f][:, 0:1024], _sc[f]))])
                    c3 = st("vector", DVE, [c2], [
                        (lambda e, f=f: e.tensor_tensor(
                            _sc[f], xhat[:, 1024:2048], _whf[f][:, 2048:3072], ALU.mult))])
                    cmf[f] = st("vector", DVE, [c3], [
                        (lambda e, f=f: e.tensor_add(
                            _yh[f][:, 1024:2048], _yh[f][:, 1024:2048], _sc[f]))])
                cm = cmf[1]
                im = st("tensor", PE, [cmf[0], state["psu_free"]], [
                    (lambda e, ns=ns, part=part: e.matmul(
                        ps_u[:, 512 * ns:512 * (ns + 1)],
                        (dftir_sb if part == 0 else dftii_sb)[:, :],
                        yhat[:, 1024 * part + 512 * ns: 1024 * part + 512 * (ns + 1)],
                        start=(part == 0), stop=(part == 1)))
                    for ns in range(2) for part in range(2)
                ])
                mfl = st("scalar", ACT, [im, state["m_mix_done"]], [
                    lambda e: e.copy(ls_tm[:], ps_u[:, 0:1024])])
                im2 = st("tensor", PE, [cm, mfl], [
                    (lambda e, ns=ns, part=part: e.matmul(
                        ps_u[:, 512 * ns:512 * (ns + 1)],
                        (dftir_sb if part == 0 else dftii_sb)[:, :],
                        yhat2[:, 1024 * part + 512 * ns: 1024 * part + 512 * (ns + 1)],
                        start=(part == 0), stop=(part == 1)))
                    for ns in range(2) for part in range(2)
                ])
                m_fir = st("scalar", ACT, [im2], [
                    lambda e: e.copy(ss_tm[:], ps_u[:, 0:1024])])
                state["psu_free"] = m_fir
                mA = st("vector", DVE, [cm, e2a, m_bgl, state["m_clamp"]], [
                    (lambda e, h=h, n=n: e.tensor_scalar(
                        mix_s[:, 256 * h:256 * (h + 1)], vtm[:, 256 * h:256 * (h + 1)],
                        sm_e[:, 12 * n + 3 * h: 12 * n + 3 * h + 1], None, ALU.mult))
                    for h in range(4)
                ])
                mB = st("vector", DVE, [mA, m_fir], [
                    (lambda e, h=h, n=n: e.scalar_tensor_tensor(
                        mix_s[:, 256 * h:256 * (h + 1)], ss_tm[:, 256 * h:256 * (h + 1)],
                        sm_e[:, 12 * n + 3 * h + 1: 12 * n + 3 * h + 2],
                        mix_s[:, 256 * h:256 * (h + 1)], ALU.mult, ALU.add))
                    for h in range(4)
                ])
                mC = st("vector", DVE, [mB, mfl], [
                    (lambda e, h=h, n=n: e.scalar_tensor_tensor(
                        mix_s[:, 256 * h:256 * (h + 1)], ls_tm[:, 256 * h:256 * (h + 1)],
                        sm_e[:, 12 * n + 3 * h + 2: 12 * n + 3 * h + 3],
                        mix_s[:, 256 * h:256 * (h + 1)], ALU.mult, ALU.add))
                    for h in range(4)
                ])
                if state["pend_tail"] is not None:
                    emit_tail()
                g1 = st("tensor", PE, [m_kl2, state["psg_free"]], [
                    (lambda e, h=h, dc=dc, cofs=cofs: e.matmul(
                        ps_g[:, 128 * h:128 * (h + 1)],
                        kh_blk[:, 512 * (2 * h + dc) + cofs: 512 * (2 * h + dc) + cofs + 128],
                        kh_blk[:, 512 * (2 * h + dc) + cofs: 512 * (2 * h + dc) + cofs + 128],
                        start=(dc == 0), stop=(dc == 1)))
                    for h in range(4) for dc in range(2)
                ])
                p1 = st("vector", DVE, [g1, m_bgl], [
                    (lambda e, h=h, bb=bb: e.scalar_tensor_tensor(
                        Pp[:, 128 * h:128 * (h + 1)], ps_g[:, 128 * h:128 * (h + 1)],
                        beta_blk[:, bb + h: bb + h + 1], masks_sb[:, 0:128],
                        ALU.mult, ALU.mult))
                    for h in range(4)
                ])
                state["psg_free"] = p1
                q1t = st("tensor", PE, [p1, state["pstr_free"]], [
                    (lambda e, h=h: e.transpose(ps_tr[:, 128 * h:128 * (h + 1)],
                                                Pp[:, 128 * h:128 * (h + 1)], eye_sb[:]))
                    for h in range(4)
                ])
                q1e = st("vector", DVE, [q1t], [
                    lambda e: e.tensor_copy(Qp[:, 0:512], ps_tr[:, 0:512]),
                    lambda e: e.tensor_tensor(Rt[:, 0:512].rearrange("p (h t) -> p h t", h=4),
                                              ps_tr[:, 0:512].rearrange("p (h t) -> p h t", h=4),
                                              eye_sb[:].unsqueeze(1)
                                              .broadcast_to([128, 4, 128]),
                                              ALU.add),
                ])
                state["pstr_free"] = q1e

                pcur = qcur = rcur = 0
                m_chain = q1e
                m_sqprev = q1e
                for lev in range(1, 7):
                    pnx, qnx, rnx = 1 - pcur, 1 - qcur, 1 - rcur
                    sq_mm = st("tensor", PE, [m_sqprev, state["pspq_free"][0],
                                              state["pspq_free"][1]], [
                        (lambda e, h=h, pc=pcur, qc=qcur: e.matmul(
                            ps_p[:, 128 * h:128 * (h + 1)],
                            Qp[:, 512 * qc + 128 * h: 512 * qc + 128 * (h + 1)],
                            Pp[:, 512 * pc + 128 * h: 512 * pc + 128 * (h + 1)],
                            start=True, stop=True))
                        for h in range(4)
                    ] + ([
                        (lambda e, h=h, pc=pcur, qc=qcur: e.matmul(
                            ps_q[:, 128 * h:128 * (h + 1)],
                            Pp[:, 512 * pc + 128 * h: 512 * pc + 128 * (h + 1)],
                            Qp[:, 512 * qc + 128 * h: 512 * qc + 128 * (h + 1)],
                            start=True, stop=True))
                        for h in range(4)
                    ] if lev < 6 else []))
                    sq_ev = st("scalar", ACT, [sq_mm], [
                        (lambda e, pn=pnx: e.copy(Pp[:, 512 * pn:512 * (pn + 1)],
                                                  ps_p[:, 0:512])),
                    ] + ([
                        (lambda e, qn=qnx: e.copy(Qp[:, 512 * qn:512 * (qn + 1)],
                                                  ps_q[:, 0:512])),
                    ] if lev < 6 else []))
                    state["pspq_free"] = [sq_ev, sq_ev]
                    m_sqprev = sq_ev
                    pr_mm = st("tensor", PE, [sq_ev, m_chain, state["psg_free"]], [
                        (lambda e, h=h, pn=pnx, rc=rcur: e.matmul(
                            ps_g[:, 128 * h:128 * (h + 1)],
                            Pp[:, 512 * pn + 128 * h: 512 * pn + 128 * (h + 1)],
                            Rt[:, 512 * rc + 128 * h: 512 * rc + 128 * (h + 1)],
                            start=True, stop=True))
                        for h in range(4)
                    ])
                    pr_ev = st("vector", DVE, [pr_mm], [
                        (lambda e, rc=rcur, rn=rnx: e.tensor_add(
                            Rt[:, 512 * rn:512 * (rn + 1)], Rt[:, 512 * rc:512 * (rc + 1)],
                            ps_g[:, 0:512]))
                    ])
                    state["psg_free"] = pr_ev
                    m_chain = pr_ev
                    pcur, qcur, rcur = pnx, qnx, rnx

                w_mm = st("tensor", PE, [m_chain, e1, state["pspq_free"][0],
                                         state["pspq_free"][1]], [
                    (lambda e, h=h, dc=dc, rc=rcur: e.matmul(
                        (ps_p if (2 * h + dc) < 4 else ps_q)
                        [:, 128 * ((2 * h + dc) % 4):128 * ((2 * h + dc) % 4 + 1)],
                        kbtm[:, 256 * h + 128 * dc: 256 * h + 128 * (dc + 1)],
                        Rt[:, 512 * rc + 128 * h: 512 * rc + 128 * (h + 1)],
                        start=True, stop=True))
                    for h in range(4) for dc in range(2)
                ])
                w_ev = st("vector", DVE, [w_mm], [
                    lambda e: e.tensor_scalar(wfm[:, 0:512], ps_p[:, 0:512], -1.0, None, ALU.mult),
                    lambda e: e.tensor_scalar(wfm[:, 512:1024], ps_q[:, 0:512], -1.0, None, ALU.mult),
                ])
                state["pspq_free"] = [w_ev, w_ev]
                u2_mm = st("tensor", PE, [m_chain, e2, w_ev, state["m_S"],
                                          state["psu_free"]], [
                    mmfn for h in range(4) for mmfn in (
                        [(lambda e, h=h, rc=rcur: e.matmul(
                            ps_u[:, 256 * h:256 * (h + 1)],
                            Rt[:, 512 * rc + 128 * h: 512 * rc + 128 * (h + 1)],
                            vbtm[:, 256 * h:256 * (h + 1)], start=True, stop=False))]
                        + [(lambda e, h=h, dc=dc, cofs=cofs: e.matmul(
                            ps_u[:, 256 * h:256 * (h + 1)],
                            wfm[:, 256 * h + 128 * dc: 256 * h + 128 * (dc + 1)],
                            S_sb[:, 256 * (2 * h + dc): 256 * (2 * h + dc) + 256],
                            start=False, stop=(dc == 1)))
                           for dc in range(2)])
                ])
                u2_ev = st("scalar", ACT, [u2_mm], [
                    lambda e: e.copy(u2_tm[:, 0:512], ps_u[:, 0:512]),
                    lambda e: e.copy(u2_tm[:, 512:1024], ps_u[:, 512:1024]),
                ])
                state["psu_free"] = u2_ev

                at_mm = st("tensor", PE, [m_ql2, m_kl2, state["psg_free"]], [
                    (lambda e, h=h, dc=dc, cofs=cofs: e.matmul(
                        ps_g[:, 128 * h:128 * (h + 1)],
                        kh_blk[:, 512 * (2 * h + dc) + cofs: 512 * (2 * h + dc) + cofs + 128],
                        qh_blk[:, 512 * (2 * h + dc) + cofs: 512 * (2 * h + dc) + cofs + 128],
                        start=(dc == 0), stop=(dc == 1)))
                    for h in range(4) for dc in range(2)
                ])
                at_ev = st("vector", DVE, [at_mm], [
                    lambda e: e.tensor_tensor(attnT.rearrange("p (h t) -> p h t", h=4),
                                              ps_g[:, 0:512].rearrange("p (h t) -> p h t", h=4),
                                              masks_sb[:, 128:256].unsqueeze(1)
                                              .broadcast_to([128, 4, 128]),
                                              ALU.mult)
                ])
                state["psg_free"] = at_ev

                o_mm = st("tensor", PE, [m_ql2, u2_ev, at_ev, state["m_S"],
                                         state["pso_free"], state["m_mix_done"]], [
                    mmfn for h in range(4) for mmfn in (
                        [(lambda e, h=h, dc=dc, cofs=cofs: e.matmul(
                            ps_o[:, 256 * h:256 * (h + 1)],
                            qh_blk[:, 512 * (2 * h + dc) + cofs: 512 * (2 * h + dc) + cofs + 128],
                            S_sb[:, 256 * (2 * h + dc): 256 * (2 * h + dc) + 256],
                            start=(dc == 0), stop=False))
                           for dc in range(2)]
                        + [(lambda e, h=h: e.matmul(
                            ps_o[:, 256 * h:256 * (h + 1)], attnT[:, 128 * h:128 * (h + 1)],
                            u2_tm[:, 256 * h:256 * (h + 1)], start=False, stop=True))])
                ])

                mD = st("vector", DVE, [o_mm, mC], [
                    (lambda e, h=h, n=n: e.scalar_tensor_tensor(
                        mix_b2[:, 256 * h:256 * (h + 1)], ps_o[:, 256 * h:256 * (h + 1)],
                        wg_t[:, 4 * n + h: 4 * n + h + 1],
                        mix_s[:, 256 * h:256 * (h + 1)], ALU.mult, ALU.add))
                    for h in range(4)
                ])
                state["pso_free"] = mD
                m_supd = []
                for dc in range(2):
                    su_mm = st("tensor", PE, [u2_ev, e1a, state["pso_free"]] + m_supd, [
                        (lambda e, h=h, dc=dc: e.matmul(
                            ps_o[:, 256 * h:256 * (h + 1)],
                            ktm[:, 256 * h + 128 * dc: 256 * h + 128 * (dc + 1)],
                            u2_tm[:, 256 * h:256 * (h + 1)], start=True, stop=True))
                        for h in range(4)
                    ])
                    su_ev = st("vector", DVE, [su_mm, state["m_S"]], [
                        (lambda e, dc=dc: e.tensor_tensor(
                            S_sb[:].rearrange("p (h d t) -> p h d t", h=4, d=2)[:, :, dc, :],
                            S_sb[:].rearrange("p (h d t) -> p h d t", h=4, d=2)[:, :, dc, :],
                            ps_o[:, 0:1024].rearrange("p (h t) -> p h t", h=4),
                            ALU.add))
                    ])
                    m_supd.append(su_ev)
                    state["pso_free"] = su_ev
                state["m_S"] = m_supd[-1]
                _sqd = [sq_scr[:], mix_s[:, 0:256], mix_s[:, 256:512], mix_s[:, 512:768]]
                mE = st("vector", DVE, [mD], [
                    (lambda e, h=h: e.scalar_tensor_tensor(
                        _sqd[h], mix_b2[:, 256 * h:256 * (h + 1)], 1.0,
                        mix_b2[:, 256 * h:256 * (h + 1)], ALU.mult, ALU.mult,
                        accum_out=rsums[:, h:h + 1]))
                    for h in range(4)
                ])
                rr0 = st("scalar", ACT, [mE], [
                    lambda e: e.activation(rstd2[:], rsums[:], AF.Ln,
                                           bias=epsc_sb[:, 0:1], scale=epsc_sb[:, 2:3])])
                rr = st("scalar", ACT, [rr0], [
                    lambda e: e.activation(rstd2[:], rstd2[:], AF.Exp, scale=epsc_sb[:, 4:5])])
                orms = st("vector", DVE, [rr], [
                    (lambda e, h=h: e.scalar_tensor_tensor(
                        o_rms[:, 256 * h:256 * (h + 1)],
                        mix_b2[:, 256 * h:256 * (h + 1)],
                        rstd2[:, h:h + 1], rms_rep_sb[:, 256 * h:256 * (h + 1)],
                        ALU.mult, ALU.mult))
                    for h in range(4)
                ])
                state["m_mix_done"] = orms
                state["pend_tail"] = {"orms": orms, "b": b, "n": n}

            emit_tail()
            ln_and_h(b, 1)
            ffn_srcs = [h_blk, qh_blk, kh_blk, y_blk]
            m_src_ready = state["m_h"]
            halves = [w_str[:, 0:1024], w_str[:, 1024:2048]]
            for fi, wname in enumerate(("wf_w", "wf_a1", "wf_a2")):
                src = ffn_srcs[fi]
                dstbuf = ffn_srcs[fi + 1]
                last_ev = None
                hfree = [state["wstr_free"], state["wstr_free"]]
                for qq in range(4):
                    dmas = []
                    for hb in range(2):
                        dmas.append(st("gpsimd", GLD, [hfree[hb]], [
                            (lambda e, wname=wname, qq=qq, hb=hb: e.dma_start(
                                halves[hb], din[wname][:, 2048 * qq + 1024 * hb:
                                                       2048 * qq + 1024 * (hb + 1)]))]))
                    mm = None
                    for hb in range(2):
                        for mh in range(2):
                            pp = (ps_p, ps_q)[mh]
                            mm = st("tensor", PE, [dmas[hb], m_src_ready,
                                                   state["pspq_free"][mh]], [
                                (lambda e, g=g, mh=mh, hb=hb, pp=pp, src=src: e.matmul(
                                    pp[:, 0:512],
                                    halves[hb][:, 256 * (g - 4 * hb) + 128 * mh:
                                               256 * (g - 4 * hb) + 128 * mh + 128],
                                    src[:, 512 * g:512 * (g + 1)],
                                    start=(g == 0), stop=(g == 7)))
                                for g in range(4 * hb, 4 * hb + 4)
                            ])
                            if mh == 1:
                                hfree[hb] = mm
                    for mh in range(2):
                        mc = 2 * qq + mh
                        pp = (ps_p, ps_q)[mh]
                        if fi < 2:
                            ev = st("scalar", ACT, [mm], [
                                (lambda e, m=mc, pp=pp, dst=dstbuf: e.copy(
                                    dst[:, 512 * m:512 * (m + 1)], pp[:, 0:512]))])
                        else:
                            ev = st("scalar", ACT, [mm, state["y_free"]], [
                                (lambda e, m=mc, pp=pp: e.activation(
                                    y_blk[:, 512 * m:512 * (m + 1)], pp[:, 0:512], AF.Tanh,
                                    bias=bias_ffn_sb[:, m:m + 1]))])
                            ev = st("vector", DVE, [ev], [
                                (lambda e, m=mc: e.tensor_scalar(
                                    y_blk[:, 512 * m:512 * (m + 1)],
                                    y_blk[:, 512 * m:512 * (m + 1)],
                                    gates_fm[:, 40 + m:41 + m], None, ALU.mult))])
                        state["pspq_free"][mh] = ev
                        last_ev = ev
                m_src_ready = last_ev
                state["wstr_free"] = mm
            state["y_free"] = m_src_ready
            for n in range(4):
                tm = st("tensor", PE, [m_src_ready, state["pstr_free"]], [
                    (lambda e, g=g, n=n: e.transpose(
                        ps_tr[:, 128 * g:128 * (g + 1)],
                        y_blk[:, 512 * g + 128 * n: 512 * g + 128 * (n + 1)], eye_sb[:]))
                    for g in range(8)
                ])
                ra = st("vector", DVE, [tm], [
                    (lambda e, b=b, n=n: e.tensor_add(
                        x_tm[:, 4096 * b + 1024 * n: 4096 * b + 1024 * (n + 1)],
                        x_tm[:, 4096 * b + 1024 * n: 4096 * b + 1024 * (n + 1)],
                        ps_tr[:, 0:1024]))
                ])
                state["pstr_free"] = ra
                state["m_x"] = ra
            st("sync", ST, [state["m_x"]], [
                (lambda e, b=b: e.dma_start(
                    out_r[:, 4 * b:4 * (b + 1), :],
                    x_tm[:, 4096 * b:4096 * (b + 1)].rearrange("p (n f) -> p n f", n=4)))])

        def fin(e):
            e.wait_ge(ST.h, ST.n)
        sch.items.append(("sync", fin))

        @block.tensor
        def _(t):
            for eng, fn in sch.items:
                if eng == "tensor":
                    fn(t)

        @block.vector
        def _(v):
            for eng, fn in sch.items:
                if eng == "vector":
                    fn(v)

        @block.scalar
        def _(s):
            for eng, fn in sch.items:
                if eng == "scalar":
                    fn(s)

        @block.gpsimd
        def _(g):
            for eng, fn in sch.items:
                if eng == "gpsimd":
                    fn(g)

        @block.sync
        def _(s):
            for eng, fn in sch.items:
                if eng == "sync":
                    fn(s)

    return nc


def kernel(**inputs):
    consts = _build_consts()
    nc = build_nc()
    in_maps = [_prep_core_inputs(inputs, b, consts) for b in range(8)]
    res = run_bass_kernel_spmd(nc, in_maps, core_ids=list(range(8)))
    return np.stack([r["out"] for r in res.results], axis=0).reshape(8, L, D)
